# version 06
# baseline (speedup 1.0000x reference)
import os
import numpy as np
import ml_dtypes
from contextlib import ExitStack
import concourse.bass as bass
import concourse.mybir as mybir
from concourse.bass_utils import run_bass_kernel_spmd

F32 = mybir.dt.float32
BF16 = mybir.dt.bfloat16
AF = mybir.ActivationFunctionType
ALU = mybir.AluOpType
AX = mybir.AxisListType.X

D = 2048
S = 2048
NC = 16
TT = 512
NTT = S // TT
EPS_LN = 1e-5

SERIAL = os.environ.get("BK_SERIAL", "0") == "1"

ENGINE_ATTR = {"pe": "tensor", "act": "scalar", "dve": "vector",
               "pool": "gpsimd", "sp": "sync"}


class Ctx:

    def __init__(self, nc, es):
        self.nc = nc
        self.es = es
        self.prog = {e: [] for e in ENGINE_ATTR}
        self.esem = {e: es.enter_context(nc.semaphore(f"es_{e}")) for e in ENGINE_ATTR}
        self.NDS = {"sp": 6, "pool": 18, "act": 4}
        self.dsem = {}
        self.dcnt = {}
        for q, n in self.NDS.items():
            for i in range(n):
                self.dsem[f"{q}{i}"] = es.enter_context(nc.semaphore(f"ds_{q}{i}"))
                self.dcnt[f"{q}{i}"] = 0
        self.didx = {q: 0 for q in self.NDS}
        self.ecnt = {e: 0 for e in ENGINE_ATTR}
        self.water = {e: {} for e in ENGINE_ATTR}
        self.key2sem = {f"e:{e}": self.esem[e] for e in ENGINE_ATTR}
        self.key2sem.update({f"d:{k}": s for k, s in self.dsem.items()})
        self.last_token = None
        self._sb = {}

    def scr_b(self, i):
        return self.sbuf(f"scr_b{i}", [128, 512], BF16)

    def scr_f(self, i):
        return self.sbuf(f"scr_f{i}", [128, 512], F32)

    def sbuf(self, name, shape, dtype):
        key = name
        if key not in self._sb:
            self._sb[key] = self.es.enter_context(
                self.nc.sbuf_tensor(name, shape, dtype))
        return self._sb[key]

    def _wait(self, cons, token):
        key, val = token
        w = self.water[cons].get(key, 0)
        if val > w and key.startswith("d:"):
            cur = self.dcnt[key[2:]]
            if val != cur:
                import sys
                print(f"WARN: stale dma token {key}@{val} (cur {cur})", file=sys.stderr)
        if val > w:
            self.prog[cons].append(("wait", self.key2sem[key], val))
            self.water[cons][key] = val

    def emit(self, e, fn, deps=(), token=False):
        if SERIAL and self.last_token is not None:
            deps = list(deps) + [self.last_token]
        for t in deps:
            if t is not None:
                self._wait(e, t)
        tok = None
        if token or SERIAL:
            self.ecnt[e] += 1
            tok = (f"e:{e}", self.ecnt[e])
            self.prog[e].append(("inst_inc", fn, self.esem[e]))
            self.last_token = tok
        else:
            self.prog[e].append(("inst", fn))
        return tok

    def dma(self, q, out_ap, in_ap, deps=(), transpose=False, accum=False):
        return self.dma_group(q, [(out_ap, in_ap)], deps=deps, transpose=transpose, accum=accum)

    def dma_group(self, q, items, deps=(), transpose=False, accum=False):
        if SERIAL and self.last_token is not None:
            deps = list(deps) + [self.last_token]
        for t in deps:
            if t is not None:
                self._wait(q, t)
        sk = f"{q}{self.didx[q]}"
        self.didx[q] = (self.didx[q] + 1) % self.NDS[q]
        kw = {"transpose": True} if transpose else {}
        if accum:
            kw["accum_op"] = ALU.add
        for out_ap, in_ap in items:
            self.dcnt[sk] += 16
            self.prog[q].append(
                ("dma", (lambda eng, o=out_ap, i=in_ap, k=kw: eng.dma_start(out=o, in_=i, **k)),
                 self.dsem[sk]))
        tok = (f"d:{sk}", self.dcnt[sk])
        self.last_token = tok
        return tok

    def fence(self, engine, tokens):
        for t in tokens:
            if t is not None:
                self._wait(engine, t)

    def barrier(self, engines=None):
        engines = engines or list(ENGINE_ATTR)
        toks = [(f"e:{e}", self.ecnt[e]) for e in ENGINE_ATTR if self.ecnt[e] > 0]
        toks += [(f"d:{k}", v) for k, v in self.dcnt.items() if v > 0]
        for e in engines:
            for t in toks:
                if t[0] != f"e:{e}":
                    self._wait(e, t)

    def build(self):
        block = self.es.enter_context(self.nc.Block())

        def mk(e):
            def body(eng):
                for item in self.prog[e]:
                    kind = item[0]
                    if kind == "wait":
                        eng.wait_ge(item[1], item[2])
                    elif kind == "inst":
                        item[1](eng)
                    elif kind == "inst_inc":
                        item[1](eng).then_inc(item[2], 1)
                    elif kind == "dma":
                        item[1](eng).then_inc(item[2], 16)
            return body

        for e, attr in ENGINE_ATTR.items():
            getattr(block, attr)(mk(e))




def ceil_div(a, b):
    return (a + b - 1) // b


class Consts:

    def __init__(self, ctx, aps):
        self.ctx = ctx
        self.ident_bf = ctx.sbuf("c_ident_bf", [128, 128], BF16)
        self.ident_f = ctx.sbuf("c_ident_f", [128, 128], F32)
        self._ones = ctx.sbuf("c_ones_row_f", [65, 128], F32)
        self.ones_row_f = self._ones[0:1, :]
        self.ones_row64 = self._ones[64:65, :]
        self.ones_col_bf = ctx.sbuf("c_ones_col_bf", [128, 1], BF16)
        t1 = ctx.dma("pool", self.ident_bf[:], aps["c_ident"])
        t2 = ctx.dma("pool", self.ident_f[:], aps["c_ident"])
        self.ready = [t1, t2]
        self.ready.append(ctx.emit("dve", lambda v: v.memset(self._ones[:], 1.0), token=True))
        self.ready.append(ctx.emit("dve", lambda v: v.memset(self.ones_col_bf[:], 1.0), token=True))


def load_pervec(ctx, consts, name, hbm_ap, n, dtype=F32):
    nchunk = n // 128
    assert nchunk <= 128
    nat = ctx.sbuf(name + "_nat", [nchunk, 128], F32)
    out = ctx.sbuf(name, [128, nchunk], F32)
    t0 = ctx.dma("pool", nat[:], hbm_ap.rearrange("(c p) -> c p", p=128))
    ps = ctx.psum_pool.get("s1")
    tp = ctx.emit("pe", lambda t: t.transpose(ps[0:128, 0:nchunk], nat[:], consts.ident_f[0:nchunk, 0:nchunk]),
                  deps=[t0] + consts.ready + ctx.psum_pool.deps("s1"), token=True)
    te = ctx.emit("dve", lambda v: v.tensor_copy(out[:], ps[0:128, 0:nchunk]),
                  deps=[tp], token=True)
    ctx.psum_pool.consumed("s1", te)
    return out, te


class PsumPool:

    def __init__(self, ctx):
        self.ctx = ctx
        self.banks = {}
        self.cons_tok = {}

    def add(self, name, shape, dtype):
        self.banks[name] = self.ctx.es.enter_context(
            self.ctx.nc.psum_tensor(f"ps_{name}", shape, dtype))

    def get(self, name):
        return self.banks[name]

    def deps(self, name):
        return self.cons_tok.get(name, [])

    def consumed(self, name, token):
        self.cons_tok[name] = [token]

    def also_consumed(self, name, token):
        self.cons_tok.setdefault(name, []).append(token)


def make_ctx(nc, es):
    ctx = Ctx(nc, es)
    pp = PsumPool(ctx)
    for nm in ("w0", "w1", "w2", "w3", "b0", "s1", "s2"):
        pp.add(nm, [128, 512], F32)
    pp.add("tp", [128, 1024], BF16)
    ctx.psum_pool = pp
    return ctx




def emit_ln(ctx, consts, src, dst, gamma_sb=None, beta_sb=None, out_f32=None):
    pp = ctx.psum_pool
    sq = ctx.scr_b(7)
    mu_b = ctx.scr_f(4)
    rs_b = ctx.scr_f(5)
    rowbuf = ctx.sbuf("scr_row", [1, 3 * TT], F32)
    mu_row = rowbuf[:, 0:TT]
    msq_row = rowbuf[:, TT:2 * TT]
    mu2_row = rowbuf[:, 2 * TT:3 * TT]
    var_row = msq_row
    sd_row = msq_row
    rstd_row = mu2_row
    t_f = ctx.scr_f(2)

    ln_prev = []
    for tt in range(NTT):
        sl = slice(tt * TT, (tt + 1) * TT)
        s1 = pp.get("s1")
        s2 = pp.get("s2")
        tok_sq_prev = None
        mm1 = mm2 = None
        for c in range(NC):
            src_ap = src(c)[:, sl]
            tsq = ctx.emit("act", lambda a, o=sq[:], i=src_ap: a.activation(o, i, AF.Square),
                           deps=[tok_sq_prev] if tok_sq_prev else [], token=True)
            mm1 = ctx.emit("pe", lambda t, o=s1[0:1, 0:TT], l=consts.ones_col_bf[:], r=src_ap,
                           st=(c == 0), sp=(c == NC - 1):
                           t.matmul(o, l, r, start=st, stop=sp),
                           deps=(pp.deps("s1") if c == 0 else []), token=(c == NC - 1))
            mm2 = ctx.emit("pe", lambda t, o=s2[0:1, 0:TT], l=consts.ones_col_bf[:], r=sq[:],
                           st=(c == 0), sp=(c == NC - 1):
                           t.matmul(o, l, r, start=st, stop=sp),
                           deps=[tsq] + (pp.deps("s2") if c == 0 else []), token=True)
            tok_sq_prev = mm2
        tmu = ctx.emit("dve", lambda v: v.tensor_scalar(mu_row, s1[0:1, 0:TT], 1.0 / D, None, ALU.mult),
                       deps=[mm1] + ln_prev, token=True)
        pp.consumed("s1", tmu)
        tms = ctx.emit("dve", lambda v: v.tensor_scalar(msq_row, s2[0:1, 0:TT], 1.0 / D, None, ALU.mult),
                       deps=[mm2], token=True)
        pp.consumed("s2", tms)
        tm2 = ctx.emit("act", lambda a: a.activation(mu2_row, mu_row, AF.Square), deps=[tmu] + ln_prev, token=True)
        tvar = ctx.emit("dve", lambda v: v.tensor_sub(var_row, msq_row, mu2_row), deps=[tms, tm2], token=True)
        tve = ctx.emit("dve", lambda v: v.tensor_scalar(var_row, var_row, EPS_LN, None, ALU.add), deps=[tvar], token=True)
        tsd = ctx.emit("act", lambda a: a.activation(sd_row, var_row, AF.Sqrt), deps=[tve], token=True)
        trs = ctx.emit("dve", lambda v: v.reciprocal(rstd_row, sd_row), deps=[tsd], token=True)
        b0 = pp.get("b0")
        w3 = pp.get("w3")
        tb1 = ctx.emit("pe", lambda t: t.matmul(b0[:, 0:TT], consts.ones_row_f, mu_row, start=True, stop=True),
                       deps=[tmu] + pp.deps("b0") + consts.ready, token=True)
        tb1e = ctx.emit("dve", lambda v: v.tensor_copy(mu_b[:], b0[:, 0:TT]), deps=[tb1] + ln_prev, token=True)
        pp.consumed("b0", tb1e)
        tb2 = ctx.emit("pe", lambda t: t.matmul(w3[:, 0:TT], consts.ones_row_f, rstd_row, start=True, stop=True),
                       deps=[trs] + pp.deps("w3"), token=True)
        tb2e = ctx.emit("dve", lambda v: v.tensor_copy(rs_b[:], w3[:, 0:TT]), deps=[tb2], token=True)
        pp.consumed("w3", tb2e)
        prev_t3 = []
        for c in range(NC):
            src_ap = src(c)[:, sl]
            if out_f32 is not None:
                dst_ap = out_f32(c)[:, sl]
            else:
                dst_ap = dst(c)[:, sl]
            t1 = ctx.emit("dve", lambda v, i=src_ap: v.tensor_sub(t_f[:], i, mu_b[:]),
                          deps=[tb1e, tb2e] + prev_t3, token=True)
            if gamma_sb is not None:
                t2 = ctx.emit("dve", lambda v: v.tensor_mul(t_f[:], t_f[:], rs_b[:]), deps=[t1], token=True)
                t3 = ctx.emit("dve", lambda v, o=dst_ap, g=gamma_sb[:, c:c + 1], b=beta_sb[:, c:c + 1]:
                              v.tensor_scalar(o, t_f[:], g, b, ALU.mult, ALU.add),
                              deps=[t2], token=True)
            else:
                t3 = ctx.emit("dve", lambda v, o=dst_ap: v.tensor_mul(o, t_f[:], rs_b[:]),
                              deps=[t1], token=True)
            prev_t3 = [t3]
        ln_prev = [t3, tsd, tb2, tb1e, tb2e]
    return t3




def build_l0_attn():
    nc = bass.Bass("TRN2", target_bir_lowering=False, debug=False)
    aps = {}
    aps["X"] = nc.dram_tensor("X", [S, D], F32, kind="ExternalInput").ap()
    aps["W_in"] = nc.dram_tensor("W_in", [D, 3 * D], F32, kind="ExternalInput").ap()
    aps["b_in"] = nc.dram_tensor("b_in", [3 * D], F32, kind="ExternalInput").ap()
    aps["W_out"] = nc.dram_tensor("W_out", [D, D], F32, kind="ExternalInput").ap()
    aps["b_out"] = nc.dram_tensor("b_out", [D], F32, kind="ExternalInput").ap()
    aps["gamma"] = nc.dram_tensor("gamma", [D], F32, kind="ExternalInput").ap()
    aps["beta"] = nc.dram_tensor("beta", [D], F32, kind="ExternalInput").ap()
    aps["c_ident"] = nc.dram_tensor("c_ident", [128, 128], F32, kind="ExternalInput").ap()
    aps["c_swap"] = nc.dram_tensor("c_swap", [128, 128], F32, kind="ExternalInput").ap()
    aps["c_cos"] = nc.dram_tensor("c_cos", [1024, S], F32, kind="ExternalInput").ap()
    aps["c_sinp"] = nc.dram_tensor("c_sinp", [1024, S], F32, kind="ExternalInput").ap()
    xout = nc.dram_tensor("xout", [NC, 128, S], BF16, kind="ExternalOutput").ap()
    xpark = nc.dram_tensor("xpark", [NC, 128, S], BF16, kind="Internal").ap()
    dbg = None
    if os.environ.get("BK_DBG", "0") == "1":
        dbg = {
            "xn": nc.dram_tensor("d_xn", [NC, 128, S], BF16, kind="ExternalOutput").ap(),
            "A": nc.dram_tensor("d_A", [128, 16], F32, kind="ExternalOutput").ap(),
            "Y": nc.dram_tensor("d_Y", [NC, 128, S], BF16, kind="ExternalOutput").ap(),
            "rows": nc.dram_tensor("d_rows", [1, 2048], F32, kind="ExternalOutput").ap(),
            "wb": nc.dram_tensor("d_wb", [128, 512], F32, kind="ExternalOutput").ap(),
            "P": nc.dram_tensor("d_P", [128, 2048], BF16, kind="ExternalOutput").ap(),
            "pw": nc.dram_tensor("d_pw", [128, 512], F32, kind="ExternalOutput").ap(),
            "acol": nc.dram_tensor("d_acol", [128, 1], F32, kind="ExternalOutput").ap(),
        }

    es = ExitStack()
    with es:
        ctx = make_ctx(nc, es)
        consts = Consts(ctx, aps)
        emit_l0_attn(ctx, consts, aps, None, xout, xpark, first=True, dbg=dbg)
        ctx.barrier(["sp"])
        ctx.build()
    return nc


def emit_l0_attn(ctx, consts, aps, xin, xout, xpark, first, dbg=None):
    nc = ctx.nc
    pp = ctx.psum_pool
    swap_bf = ctx.sbuf("c_swap_bf", [128, 128], BF16)
    tsw = ctx.dma("pool", swap_bf[:], aps["c_swap"])

    xn = ctx.sbuf("big_a", [128, NC * S], BF16)
    xn_c = lambda c: xn[:, c * S:(c + 1) * S]
    x = ctx.sbuf("big_b", [128, NC * S], BF16)
    x_c = lambda c: x[:, c * S:(c + 1) * S]
    big_c = ctx.sbuf("big_c", [128, 20480], BF16)

    ent = big_c[:, 16384:16384 + D]
    entry_done = []
    x3 = x.rearrange("p (c s) -> p c s", c=NC)
    if first:
        for sc in range(16):
            tld = ctx.dma("pool", ent[:], aps["X"][sc * 128:(sc + 1) * 128, :],
                          deps=entry_done)
            entry_done = []
            tpb = pp.get("tp")
            for half in range(2):
                ttp = None
                for k in range(8):
                    dc = half * 8 + k
                    ttp = ctx.emit("pe", lambda t, i=ent[:, dc * 128:(dc + 1) * 128],
                                   o=tpb[:, k * 128:(k + 1) * 128]:
                                   t.transpose(o, i, consts.ident_bf[:]),
                                   deps=[tld] + consts.ready + (pp.deps("tp") if k == 0 else []),
                                   token=(k == 7))
                tev = ctx.emit("act", lambda a,
                               o=x3[:, half * 8:(half + 1) * 8, sc * 128:(sc + 1) * 128],
                               i=tpb[:].rearrange("p (k f) -> p k f", k=8):
                               a.copy(o, i),
                               deps=[ttp], token=True)
                entry_done.append(tev)
                pp.consumed("tp", tev)
    else:
        for c in range(NC):
            entry_done.append(ctx.dma("pool", x_c(c), xin[c]))

    park_tok = [ctx.dma_group("pool", [(xpark[c], x_c(c)) for c in range(NC)],
                              deps=entry_done)]

    gamma_sb, tg = load_pervec(ctx, consts, "v_gamma", aps["gamma"], D)
    beta_sb, tb = load_pervec(ctx, consts, "v_beta", aps["beta"], D)
    bin_sb, tbin = load_pervec(ctx, consts, "v_bin", aps["b_in"], 3 * D)
    bout_sb, tbo = load_pervec(ctx, consts, "v_bout", aps["b_out"], D)
    ctx.barrier(["pe", "dve", "act"])
    tln = emit_ln(ctx, consts, x_c, xn_c, gamma_sb, beta_sb)
    ctx.barrier(["pe", "dve", "act", "pool", "sp"])
    if dbg is not None:
        for c in range(NC):
            ctx.dma("pool", dbg["xn"][c], xn_c(c))

    A_acc = ctx.sbuf("A_acc", [128, 16], F32)
    tA = ctx.emit("dve", lambda v: v.memset(A_acc[:], 0.0), token=True)
    kw = big_c[:, 0:2 * NC * 512]
    kw2 = x[:, 16384:32768]
    kw_k = lambda d_: kw[:, d_ * 512:(d_ + 1) * 512]
    kw_v = lambda d_: kw[:, (NC + d_) * 512:(NC + d_ + 1) * 512]
    kw_kp = lambda p_, d_: (kw, kw2)[p_][:, d_ * 512:(d_ + 1) * 512]
    kw_vp = lambda p_, d_: (kw, kw2)[p_][:, (NC + d_) * 512:(NC + d_ + 1) * 512]
    ktile_q = (ctx.scr_b(0), ctx.scr_b(8))
    vtile_q = (ctx.scr_b(1), ctx.scr_b(9))
    krot_q = (ctx.scr_b(2), ctx.scr_b(10))
    vrot_q = (ctx.scr_b(3), ctx.scr_b(11))
    costl_q = (ctx.scr_b(4), ctx.scr_b(4))
    sintl_q = (ctx.scr_b(5), ctx.scr_b(5))
    tf1_q = (ctx.scr_f(0), ctx.scr_f(0))
    tf2_q = (ctx.scr_f(1), ctx.scr_f(1))
    sqt_q = (ctx.scr_b(6), ctx.scr_b(12))
    pw = ctx.scr_f(2)
    acol = ctx.sbuf("kv_acol", [128, 1], F32)
    wrow3 = ctx.sbuf("scr_row", [1, 3 * 512], F32)
    w_b = ctx.scr_f(3)
    p_c = lambda j, ts: x[:, j * 1024 + ts * 512: j * 1024 + (ts + 1) * 512]
    ctx.fence("dve", park_tok)
    sq_banks = {(0, 0): "s1", (0, 1): "s2", (1, 0): "b0", (1, 1): "w3"}

    last_kw_use = {0: [], 1: []}
    last_p_use = [tA]
    cos_use = {0: [], 1: []}
    fin_prev = []
    fin_last = []
    rot_prev = {0: [], 1: []}
    kv_lds = {}

    def kv_load(gi):
        jg_ = gi % 4
        p_ = gi % 2
        lk = ctx.dma_group("pool", [
            (kw_kp(p_, d_), aps["W_in"][d_ * 128:(d_ + 1) * 128, D + jg_ * 512: D + (jg_ + 1) * 512])
            for d_ in range(NC)], deps=last_kw_use[p_] + park_tok)
        lv = ctx.dma_group("pool", [
            (kw_vp(p_, d_), aps["W_in"][d_ * 128:(d_ + 1) * 128, 2 * D + jg_ * 512: 2 * D + (jg_ + 1) * 512])
            for d_ in range(NC)])
        last_kw_use[p_] = []
        kv_lds[gi] = (lk, lv)

    kv_load(0)
    kv_load(1)
    for tpass in range(2 if os.environ.get('BK_TPASS', '2') == '2' else 1):
        sq_done = {}
        for jg in range(4):
            gi = tpass * 4 + jg
            p_ = gi % 2
            if gi + 1 <= 7 and gi + 1 not in kv_lds:
                kv_load(gi + 1)
            ld_k, ld_v = kv_lds[gi]
            for jj in range(4):
                pair = jg * 4 + jj
                for tsub in range(2):
                    q_ = (pair * 2 + tsub) % 2
                    ktile = ktile_q[q_]
                    vtile = vtile_q[q_]
                    krot = krot_q[q_]
                    vrot = vrot_q[q_]
                    costl = costl_q[q_]
                    sintl = sintl_q[q_]
                    sqt = sqt_q[q_]
                    tf1 = tf1_q[q_]
                    tf2 = tf2_q[q_]
                    tt = tpass * 2 + tsub
                    sl = slice(tt * 512, (tt + 1) * 512)
                    pk = pp.get("w0")
                    pv = pp.get("w1")
                    mmk = mmv = None
                    for d_ in range(NC):
                        mmk = ctx.emit("pe", lambda t, o=pk[:], l=kw_kp(p_, d_)[:, jj * 128:(jj + 1) * 128],
                                       r=xn_c(d_)[:, sl], st=(d_ == 0), sp=(d_ == NC - 1):
                                       t.matmul(o, l, r, start=st, stop=sp),
                                       deps=([ld_k, tln] + (pp.deps("w0") if d_ == 0 else [])),
                                       token=(d_ == NC - 1))
                    for d_ in range(NC):
                        mmv = ctx.emit("pe", lambda t, o=pv[:], l=kw_vp(p_, d_)[:, jj * 128:(jj + 1) * 128],
                                       r=xn_c(d_)[:, sl], st=(d_ == 0), sp=(d_ == NC - 1):
                                       t.matmul(o, l, r, start=st, stop=sp),
                                       deps=([ld_v] + (pp.deps("w1") if d_ == 0 else [])),
                                       token=(d_ == NC - 1))
                    last_kw_use[p_].extend([mmk, mmv])
                    tke = ctx.emit("act", lambda a, o=ktile[:], p=pk[:], b=bin_sb[:, 16 + pair:17 + pair]:
                                   a.activation(o, p, AF.Identity, bias=b),
                                   deps=[mmk, tbin] + rot_prev[q_] + last_p_use, token=True)
                    pp.consumed("w0", tke)
                    tve = ctx.emit("act", lambda a, o=vtile[:], p=pv[:], b=bin_sb[:, 32 + pair:33 + pair]:
                                   a.activation(o, p, AF.Identity, bias=b),
                                   deps=[mmv, tbin], token=True)
                    pp.consumed("w1", tve)
                    if pair < 8:
                        tlc = ctx.dma_group("pool", [
                            (costl[:], aps["c_cos"][pair * 128:(pair + 1) * 128, sl]),
                            (sintl[:], aps["c_sinp"][pair * 128:(pair + 1) * 128, sl])],
                            deps=cos_use[0] + cos_use[1])
                        tls = tlc
                        cos_use[0] = []
                        cos_use[1] = []
                        psw = pp.get("w2")
                        tkv_ready = []
                        for (tin, tout) in ((ktile, krot), (vtile, vrot)):
                            tswm = ctx.emit("pe", lambda t, i=tin[:]:
                                            t.matmul(psw[:], swap_bf[:], i, start=True, stop=True),
                                            deps=[tke, tve, tsw] + pp.deps("w2"), token=True)
                            t1 = ctx.emit("dve", lambda v, o=tf1[:], s=sintl[:]: v.tensor_mul(o, psw[:], s),
                                          deps=[tswm, tls] + rot_prev[0] + rot_prev[1], token=True)
                            pp.consumed("w2", t1)
                            t2 = ctx.emit("dve", lambda v, i=tin[:], o=tf2[:], s=costl[:]: v.tensor_mul(o, i, s),
                                          deps=[tlc] + rot_prev[0] + rot_prev[1], token=True)
                            t3 = ctx.emit("dve", lambda v, o=tout[:], a_=tf1[:], b_=tf2[:]: v.tensor_add(o, a_, b_),
                                          deps=[t1, t2], token=True)
                            tkv_ready.append(t3)
                            cos_use[0].append(t2)
                            rot_prev[q_] = [t3]
                        ksrc, vsrc = krot, vrot
                    else:
                        tkv_ready = [tke, tve]
                        ksrc, vsrc = ktile, vtile
                    first_pair = (pair == 0)
                    last_pair = (pair == 15)
                    for kv_i, tsrc in ((0, ksrc), (1, vsrc)):
                        bname = sq_banks[(kv_i, tsub)]
                        sb_ = pp.get(bname)
                        tsq = ctx.emit("act", lambda a, i=tsrc[:], o=sqt[:]: a.activation(o, i, AF.Square),
                                       deps=tkv_ready + ([sq_done.get(("mm", q_))] if sq_done.get(("mm", q_)) else []),
                                       token=True)
                        msq = ctx.emit("pe", lambda t, o=sb_[0:1, 0:512], i=sqt[:], st=first_pair, sp=last_pair:
                                       t.matmul(o, consts.ones_col_bf[:], i, start=st, stop=sp),
                                       deps=[tsq] + (pp.deps(bname) if first_pair else []),
                                       token=True)
                        sq_done[("mm", q_)] = msq
                        if last_pair:
                            sq_done[(kv_i, tsub)] = msq
                    tp_ = ctx.emit("dve", lambda v, a=ksrc[:], b=vsrc[:], o=p_c(pair, tsub):
                                   v.tensor_mul(o, a, b), deps=tkv_ready + last_p_use, token=True)
                    last_p_use = []
                    rot_prev[q_] = rot_prev[q_] + [tp_]
                    last_kw_use[p_].append(tp_)
        for tsub in range(2):
            skb = pp.get(sq_banks[(0, tsub)])
            svb = pp.get(sq_banks[(1, tsub)])
            r1 = wrow3[:, 0:512]
            r2 = wrow3[:, 512:1024]
            r3 = wrow3[:, 1024:1536]
            ts1 = ctx.emit("act", lambda a, i=skb[0:1, 0:512]: a.activation(r1, i, AF.Sqrt),
                           deps=[sq_done[(0, tsub)]] + fin_prev, token=True)
            ts2 = ctx.emit("act", lambda a, i=svb[0:1, 0:512]: a.activation(r2, i, AF.Sqrt),
                           deps=[sq_done[(1, tsub)]] + fin_prev, token=True)
            tr1 = ctx.emit("dve", lambda v: v.reciprocal(r1, r1), deps=[ts1], token=True)
            pp.consumed(sq_banks[(0, tsub)], tr1)
            tr2 = ctx.emit("dve", lambda v: v.reciprocal(r2, r2), deps=[ts2], token=True)
            pp.consumed(sq_banks[(1, tsub)], tr2)
            tw = ctx.emit("dve", lambda v: v.tensor_mul(r3, r1, r2), deps=[tr1, tr2], token=True)
            fin_prev = [tw]
            bb = pp.get("w2")
            tbw = ctx.emit("pe", lambda t: t.matmul(bb[:], consts.ones_row_f, r3, start=True, stop=True),
                           deps=[tw] + pp.deps("w2"), token=True)
            tbe = ctx.emit("dve", lambda v: v.tensor_copy(w_b[:], bb[:]), deps=[tbw] + fin_last, token=True)
            pp.consumed("w2", tbe)
            pair_range = range(15, 16) if os.environ.get("BK_NPAIR") == "1" else range(16)
            for pair in pair_range:
                t1 = ctx.emit("dve", lambda v, p=p_c(pair, tsub): v.tensor_mul(pw[:], p, w_b[:]),
                              deps=[tbe] + fin_last, token=True)
                t2 = ctx.emit("dve", lambda v: v.reduce_sum(acol[:], pw[:], axis=AX), deps=[t1], token=True)
                t3 = ctx.emit("dve", lambda v, o=A_acc[:, pair:pair + 1]:
                              v.tensor_add(o, o, acol[:]), deps=[t2], token=True)
                last_p_use.append(t1)
                fin_last = [t3]
    if dbg is not None:
        ctx.barrier(["pool", "dve"])
        hosts = [tf1, tf2, pw, krot]
        for i, bn in enumerate(["s1", "s2", "b0", "w3"]):
            ctx.emit("dve", lambda v, o=hosts[i][0:1, 0:512], p=pp.get(bn)[0:1, 0:512]:
                     v.tensor_copy(o, p), token=True)
        ctx.barrier(["pool"])
        for i in range(4):
            ctx.dma("pool", dbg["rows"][:, i * 512:(i + 1) * 512], hosts[i][0:1, 0:512])
        ctx.dma("pool", dbg["wb"], w_b[:])
        ctx.dma("pool", dbg["pw"], pw[:])
        ctx.dma("pool", dbg["acol"], acol[:])
        ctx.dma("pool", dbg["P"], x[:, 0:2048])
        ctx.dma("pool", dbg["A"], A_acc[:])
        ctx.barrier(["dve"])
    tA = ctx.emit("dve", lambda v: v.tensor_scalar(A_acc[:], A_acc[:], 0.5, 0.5, ALU.mult, ALU.add),
                  deps=fin_last, token=True)

    y = x
    y_c = x_c
    last_kw_all = last_kw_use[0] + last_kw_use[1]
    for fw in range(4):
        ld_q = ctx.dma_group("pool", [
            (kw_k(d_), aps["W_in"][d_ * 128:(d_ + 1) * 128, fw * 512:(fw + 1) * 512])
            for d_ in range(NC)], deps=last_kw_all + park_tok)
        last_kw_all = []
        for ff in range(4):
            fq = fw * 4 + ff
            for tt in range(NTT):
                sl = slice(tt * TT, (tt + 1) * TT)
                bank = ("w0", "w1")[(ff * NTT + tt) % 2]
                pq = pp.get(bank)
                mmq = None
                for d_ in range(NC):
                    mmq = ctx.emit("pe", lambda t, o=pq[:], l=kw_k(d_)[:, ff * 128:(ff + 1) * 128],
                                   r=xn_c(d_)[:, sl], st=(d_ == 0), sp=(d_ == NC - 1):
                                   t.matmul(o, l, r, start=st, stop=sp),
                                   deps=([ld_q, tA] + (pp.deps(bank) if d_ == 0 else [])),
                                   token=(d_ == NC - 1))
                last_kw_all.append(mmq)
                tev = ctx.emit("dve", lambda v, o=y_c(fq)[:, sl], b=bin_sb[:, fq:fq + 1],
                               a=A_acc[:, fq:fq + 1], p=pq[:]:
                               v.tensor_scalar(o, p, b, a, ALU.add, ALU.mult),
                               deps=[mmq, tA], token=True)
                pp.consumed(bank, tev)

    if dbg is not None:
        for c in range(NC):
            ctx.dma("pool", dbg["Y"][c], y_c(c))
    ctx.barrier(["pe"])
    xnew = xn
    xnew_c = xn_c
    xo0 = ctx.scr_b(4)
    xo1 = ctx.scr_b(5)
    t1f = ctx.scr_f(3)
    xo_use = {0: [], 1: []}
    prev_tb = []
    for fw in range(4):
        ld_o = ctx.dma_group("pool", [
            (kw_v(d_), aps["W_out"][d_ * 128:(d_ + 1) * 128, fw * 512:(fw + 1) * 512])
            for d_ in range(NC)], deps=last_kw_all)
        last_kw_all = []
        for ff in range(4):
            fo = fw * 4 + ff
            for tt in range(NTT):
                sl = slice(tt * TT, (tt + 1) * TT)
                par = (ff * NTT + tt) % 2
                xo = (xo0, xo1)[par]
                tldx = ctx.dma("pool", xo[:], xpark[fo, :, sl], deps=xo_use[par])
                xo_use[par] = []
                bank = ("w0", "w1")[par]
                po = pp.get(bank)
                mmo = None
                for d_ in range(NC):
                    mmo = ctx.emit("pe", lambda t, o=po[:], l=kw_v(d_)[:, ff * 128:(ff + 1) * 128],
                                   r=y_c(d_)[:, sl], st=(d_ == 0), sp=(d_ == NC - 1):
                                   t.matmul(o, l, r, start=st, stop=sp),
                                   deps=([ld_o] + (pp.deps(bank) if d_ == 0 else [])),
                                   token=(d_ == NC - 1))
                last_kw_all.append(mmo)
                ta = ctx.emit("act", lambda a, p=po[:], b=bout_sb[:, fo:fo + 1]:
                              a.activation(t1f[:], p, AF.Identity, bias=b),
                              deps=[mmo, tbo] + prev_tb, token=True)
                pp.consumed(bank, ta)
                tb_ = ctx.emit("dve", lambda v, o=xnew_c(fo)[:, sl], xold=xo[:]:
                               v.scalar_tensor_tensor(o, xold, 2.0, t1f[:], ALU.mult, ALU.add),
                               deps=[ta, tldx], token=True)
                xo_use[par].append(tb_)
                prev_tb = [tb_]

    ctx.barrier(["pe", "dve", "act"])
    emit_ln(ctx, consts, xnew_c, xnew_c)
    if xout is not None:
        ctx.barrier(["pool"])
        ctx.dma_group("pool", [(xout[c], xnew_c(c)) for c in range(NC)])





def np_consts():
    ident = np.eye(128, dtype=np.float32)
    swap = np.zeros((128, 128), dtype=np.float32)
    m = np.arange(128)
    swap[m ^ 1, m] = 1.0
    return ident, swap


def np_rotary_tables():
    rot = D // 2
    inv = 1.0 / (10000.0 ** (np.arange(0, rot, 2, dtype=np.float64) / rot))
    f = np.arange(S, dtype=np.float64)[None, :] * inv[:, None]
    cos = np.repeat(np.cos(f), 2, axis=0).astype(np.float32)
    sin = np.repeat(np.sin(f), 2, axis=0)
    sign = np.where(np.arange(rot) % 2 == 0, -1.0, 1.0)[:, None]
    sinp = (sin * sign).astype(np.float32)
    return np.ascontiguousarray(cos), np.ascontiguousarray(sinp)


def run_l0_attn(inputs, n_cores=8):
    nc = build_l0_attn()
    ident, swap = np_consts()
    cos, sinp = np_rotary_tables()
    X = np.ascontiguousarray(inputs["L0_attn_X"])
    in_maps = []
    for b in range(n_cores):
        in_maps.append({
            "X": X[b],
            "W_in": inputs["L0_attn_W_in"],
            "b_in": inputs["L0_attn_b_in"],
            "W_out": inputs["L0_attn_W_out"],
            "b_out": inputs["L0_attn_b_out"],
            "gamma": inputs["L0_attn_gamma"],
            "beta": inputs["L0_attn_beta"],
            "c_ident": ident, "c_swap": swap, "c_cos": cos, "c_sinp": sinp,
        })
    res = run_bass_kernel_spmd(nc, in_maps, core_ids=list(range(n_cores)))
    return [r["xout"] for r in res.results]


def build_fused():
    nc = bass.Bass("TRN2", target_bir_lowering=False, debug=False)
    aps = {}
    def inp(name, shape):
        aps[name] = nc.dram_tensor(name, shape, F32, kind="ExternalInput").ap()
    inp("X", [S, D]); inp("W_in", [D, 3 * D]); inp("b_in", [3 * D])
    inp("W_out", [D, D]); inp("b_out", [D]); inp("gamma", [D]); inp("beta", [D])
    inp("W_up", [INTER, D]); inp("W_gate", [INTER, D]); inp("W_down", [D, INTER])
    inp("SU_ug", [D]); inp("SU_dp", [INTER]); inp("Wscale_ug", [2 * INTER]); inp("Wscale_dp", [D])
    inp("memories", [M, H, HD, HD]); inp("memory_norms", [M, H, HD])
    inp("w_q", [D, D]); inp("w_k", [D, D]); inp("w_v", [D, D]); inp("w_o", [D, D]); inp("gate", [H])
    inp("wg", [NEXP, D]); inp("w1", [NEXP, D, 2 * FFN]); inp("w2", [NEXP, FFN, D])
    inp("c_ident", [128, 128]); inp("c_swap", [128, 128])
    inp("c_cos", [1024, S]); inp("c_sinp", [1024, S])
    inp("c_h128", [128, 128]); inp("c_maskT", [128, 128])
    out = nc.dram_tensor("out", [S, D], F32, kind="ExternalOutput").ap()
    xpark = nc.dram_tensor("xpark", [NC, 128, S], BF16, kind="Internal").ap()
    wd2_hbm = nc.dram_tensor("wd2_hbm", [16, 128, INTER], BF16, kind="Internal").ap()
    za_hbm = nc.dram_tensor("za_hbm", [NTT, 128, NCI * TT], BF16, kind="Internal").ap()
    acc_hbm = nc.dram_tensor("acc_hbm", [NC, 128, S], F32, kind="Internal").ap()
    crow_hbm = nc.dram_tensor("crow_hbm", [NEXP, S], F32, kind="Internal").ap()
    es = ExitStack()
    with es:
        ctx = make_ctx(nc, es)
        consts = Consts(ctx, aps)
        emit_l0_attn(ctx, consts, aps, None, None, xpark, first=True)
        ctx.barrier()
        emit_l0_ffn(ctx, consts, aps, wd2_hbm, za_hbm)
        ctx.barrier()
        emit_l1_attn(ctx, consts, aps)
        ctx.barrier()
        emit_l1_moe(ctx, consts, aps, acc_hbm, crow_hbm)
        emit_final_out(ctx, consts, out, acc_hbm)
        ctx.barrier(["sp"])
        ctx.build()
    return nc


def kernel_fused(**inputs):
    n = 8
    nc = build_fused()
    ident, swap = np_consts()
    cos, sinp = np_rotary_tables()
    X = np.ascontiguousarray(np.asarray(inputs["L0_attn_X"]))
    in_maps = []
    for b in range(n):
        in_maps.append({
            "X": X[b],
            "W_in": inputs["L0_attn_W_in"], "b_in": inputs["L0_attn_b_in"],
            "W_out": inputs["L0_attn_W_out"], "b_out": inputs["L0_attn_b_out"],
            "gamma": inputs["L0_attn_gamma"], "beta": inputs["L0_attn_beta"],
            "W_up": inputs["L0_ffn_W_up"], "W_gate": inputs["L0_ffn_W_gate"],
            "W_down": inputs["L0_ffn_W_down"],
            "SU_ug": inputs["L0_ffn_SU_ug"], "SU_dp": inputs["L0_ffn_SU_dp"],
            "Wscale_ug": inputs["L0_ffn_Wscale_ug"], "Wscale_dp": inputs["L0_ffn_Wscale_dp"],
            "memories": inputs["L1_attn_memories"], "memory_norms": inputs["L1_attn_memory_norms"],
            "w_q": inputs["L1_attn_w_q"], "w_k": inputs["L1_attn_w_k"],
            "w_v": inputs["L1_attn_w_v"], "w_o": inputs["L1_attn_w_o"],
            "gate": inputs["L1_attn_gate"],
            "wg": inputs["L1_ffn_wg"], "w1": inputs["L1_ffn_w1"], "w2": inputs["L1_ffn_w2"],
            "c_ident": ident, "c_swap": swap, "c_cos": cos, "c_sinp": sinp,
            "c_h128": np_h128(), "c_maskT": np_maskT(),
        })
    res = run_bass_kernel_spmd(nc, in_maps, core_ids=list(range(n)))
    return np.stack([np.asarray(r["out"], dtype=np.float32) for r in res.results], axis=0)


def kernel(**inputs):
    if os.environ.get("BK_UNFUSED", "0") != "1":
        return kernel_fused(**inputs)
    n = 8
    X = np.ascontiguousarray(np.asarray(inputs["L0_attn_X"]))
    x1 = run_l0_attn(inputs, n_cores=n)
    x2 = run_l0_ffn(x1, inputs, n_cores=n)
    x3 = run_l1_attn(x2, inputs, n_cores=n)
    outs = run_l1_moe(x3, inputs, n_cores=n)
    return np.stack([np.asarray(o, dtype=np.float32) for o in outs], axis=0)



INTER = 8192
NCI = INTER // 128


def build_l0_ffn():
    nc = bass.Bass("TRN2", target_bir_lowering=False, debug=False)
    aps = {}
    aps["W_up"] = nc.dram_tensor("W_up", [INTER, D], F32, kind="ExternalInput").ap()
    aps["W_gate"] = nc.dram_tensor("W_gate", [INTER, D], F32, kind="ExternalInput").ap()
    aps["W_down"] = nc.dram_tensor("W_down", [D, INTER], F32, kind="ExternalInput").ap()
    aps["SU_ug"] = nc.dram_tensor("SU_ug", [D], F32, kind="ExternalInput").ap()
    aps["SU_dp"] = nc.dram_tensor("SU_dp", [INTER], F32, kind="ExternalInput").ap()
    aps["Wscale_ug"] = nc.dram_tensor("Wscale_ug", [2 * INTER], F32, kind="ExternalInput").ap()
    aps["Wscale_dp"] = nc.dram_tensor("Wscale_dp", [D], F32, kind="ExternalInput").ap()
    aps["c_ident"] = nc.dram_tensor("c_ident", [128, 128], F32, kind="ExternalInput").ap()
    aps["c_h128"] = nc.dram_tensor("c_h128", [128, 128], F32, kind="ExternalInput").ap()
    xin = nc.dram_tensor("xin", [NC, 128, S], BF16, kind="ExternalInput").ap()
    xout = nc.dram_tensor("xout", [NC, 128, S], BF16, kind="ExternalOutput").ap()
    wd2_hbm = nc.dram_tensor("wd2_hbm", [16, 128, INTER], BF16, kind="Internal").ap()
    za_hbm = nc.dram_tensor("za_hbm", [NTT, 128, NCI * TT], BF16, kind="Internal").ap()

    es = ExitStack()
    with es:
        ctx = make_ctx(nc, es)
        consts = Consts(ctx, aps)
        x = ctx.sbuf("big_a", [128, NC * S], BF16)
        x_c = lambda c: x[:, c * S:(c + 1) * S]
        ld = ctx.dma_group("pool", [(x_c(c), xin[c]) for c in range(NC)])
        ctx.barrier()
        emit_l0_ffn(ctx, consts, aps, wd2_hbm, za_hbm)
        ctx.barrier(["pool"])
        ctx.dma_group("pool", [(xout[c], x_c(c)) for c in range(NC)])
        ctx.barrier(["sp"])
        ctx.build()
    return nc


def emit_l0_ffn(ctx, consts, aps, wd2_hbm, za_hbm):
    nc = ctx.nc
    pp = ctx.psum_pool
    x = ctx.sbuf("big_a", [128, NC * S], BF16)
    x_c = lambda c: x[:, c * S:(c + 1) * S]
    bb = ctx.sbuf("big_b", [128, NC * S], BF16)
    bc = ctx.sbuf("big_c", [128, 20480], BF16)
    pw_a = bb[:, 0:16384].bitcast(F32)
    pw_b = bc[:, 0:16384].bitcast(F32)
    stW1 = bb[:, 16384:24576]
    stW2 = bb[:, 24576:32768]
    TTF = 256
    p1_a = bc[:, 0:8192].bitcast(F32)
    p1_b = bc[:, 8192:16384].bitcast(F32)
    x1_c = lambda c: bb[:, c * S:(c + 1) * S]
    za_c = lambda c: bb[:, c * TT:(c + 1) * TT]
    h128 = ctx.sbuf("c_h128_bf", [128, 128], BF16)
    th = ctx.dma("pool", h128[:], aps["c_h128"])

    su_ug, t1v = load_pervec(ctx, consts, "v_suug", aps["SU_ug"], D)
    wsug, t2v = load_pervec(ctx, consts, "v_wsug", aps["Wscale_ug"], 2 * INTER)
    sudp, t3v = load_pervec(ctx, consts, "v_sudp", aps["SU_dp"], INTER)
    wsdp, t4v = load_pervec(ctx, consts, "v_wsdp", aps["Wscale_dp"], D)
    su2 = ctx.sbuf("v_su2", [128, NCI], F32)
    sg = ctx.sbuf("v_sg", [128, NCI], F32)
    sdp = ctx.sbuf("v_sdp", [128, NC], F32)
    c2048 = 1.0 / np.sqrt(2048.0)
    c8192 = 1.0 / np.sqrt(8192.0)
    tv1 = ctx.emit("dve", lambda v: v.tensor_scalar(su2[:], wsug[:, 0:NCI], c2048, None, ALU.mult),
                   deps=[t2v], token=True)
    tv2 = ctx.emit("dve", lambda v: v.tensor_mul(su2[:], su2[:], sudp[:]), deps=[tv1, t3v], token=True)
    tv3 = ctx.emit("dve", lambda v: v.tensor_scalar(sg[:], wsug[:, NCI:2 * NCI], c2048, None, ALU.mult),
                   deps=[t2v], token=True)
    tv4 = ctx.emit("dve", lambda v: v.tensor_scalar(sdp[:], wsdp[:], c8192, None, ALU.mult),
                   deps=[t4v], token=True)

    def butterfly(buf_cur, buf_nxt, h, fcols, deps, tag):
        cv = buf_cur.rearrange("p (a two b f) -> p a two b f", two=2, b=h, f=fcols)
        nv = buf_nxt.rearrange("p (a two b f) -> p a two b f", two=2, b=h, f=fcols)
        hf = fcols // 2
        toks = []
        for sign in range(2):
            for half in range(2):
                cs = slice(half * hf, (half + 1) * hf)
                eng = ("dve", "pool")[(sign + half) % 2]
                if sign == 0:
                    fn = (lambda E, s=cs: E.tensor_add(nv[:, :, 0, :, s], cv[:, :, 0, :, s], cv[:, :, 1, :, s]))
                else:
                    fn = (lambda E, s=cs: E.tensor_sub(nv[:, :, 1, :, s], cv[:, :, 0, :, s], cv[:, :, 1, :, s]))
                toks.append(ctx.emit(eng, fn, deps=deps, token=True))
        return toks

    last_pwa_use = []
    last_st_use = []
    for rc in range(16):
        ldw = ctx.dma("pool", pw_a, aps["W_down"][rc * 128:(rc + 1) * 128, :],
                      deps=last_pwa_use)
        toks = [ldw]
        cur, nxt = pw_a, pw_b
        for st in range(6):
            toks = butterfly(cur, nxt, 1 << st, 128, toks, f"wd{rc}_{st}")
            cur, nxt = nxt, cur
        tc1 = ctx.emit("dve", lambda v: v.tensor_copy(stW1[:, 0:4096], pw_a[:, 0:4096]),
                       deps=toks + last_st_use, token=True)
        tc2 = ctx.emit("dve", lambda v: v.tensor_copy(stW1[:, 4096:8192], pw_a[:, 4096:8192]),
                       deps=toks, token=True)
        last_pwa_use = [tc1, tc2]
        tpb = pp.get("tp")
        ev_toks = []
        for g8 in range(8):
            ttp = None
            for k in range(8):
                c = g8 * 8 + k
                ttp = ctx.emit("pe", lambda t, i=stW1[:, c * 128:(c + 1) * 128],
                               o=tpb[:, k * 128:(k + 1) * 128]:
                               t.transpose(o, i, consts.ident_bf[:]),
                               deps=[tc1, tc2] + consts.ready + (pp.deps("tp") if k == 0 else []),
                               token=(k == 7))
            tev = ctx.emit("act", lambda a,
                           o=stW2[:, g8 * 1024:(g8 + 1) * 1024],
                           i=tpb[:, 0:1024]: a.copy(o, i),
                           deps=[ttp] + last_st_use, token=True)
            pp.consumed("tp", tev)
            ev_toks.append(tev)
        tout = ctx.dma("pool", wd2_hbm[rc], stW2, deps=ev_toks)
        last_st_use = [tout]

    ctx.barrier(["pe", "dve", "act", "pool"])
    xsu = ctx.sbuf("scr_b256", [128, TTF], BF16)
    x1_done = []
    x1_prev = []
    for tt in range(S // TTF):
        sl = slice(tt * TTF, (tt + 1) * TTF)
        ev_prev = []
        prev_xsu = []
        for c in range(NC):
            bank = ("w0", "w1")[c % 2]
            ps = pp.get(bank)
            tx = ctx.emit("dve", lambda v, i=x_c(c)[:, sl], s=su_ug[:, c:c + 1]:
                          v.tensor_scalar(xsu[:], i, s, None, ALU.mult),
                          deps=[t1v] + prev_xsu, token=True)
            mmh = ctx.emit("pe", lambda t, o=ps[:, 0:TTF]: t.matmul(o, h128[:], xsu[:], start=True, stop=True),
                           deps=[tx, th] + pp.deps(bank), token=True)
            prev_xsu = [mmh]
            tev = ctx.emit("dve", lambda v, o=p1_a[:, c * TTF:(c + 1) * TTF], i=ps[:, 0:TTF]:
                           v.tensor_copy(o, i), deps=[mmh] + x1_prev, token=True)
            pp.consumed(bank, tev)
            ev_prev.append(tev)
        toks = ev_prev
        cur, nxt = p1_a, p1_b
        for st in range(4):
            toks = butterfly(cur, nxt, 1 << st, TTF, toks, f"x1_{tt}_{st}")
            cur, nxt = nxt, cur
        for c in range(NC):
            tcx = ctx.emit("dve", lambda v, o=x1_c(c)[:, sl], i=p1_a[:, c * TTF:(c + 1) * TTF]:
                           v.tensor_copy(o, i), deps=toks, token=True)
            x1_done.append(tcx)
        x1_prev = [x1_done[-1]]

    ctx.barrier(["pe", "dve", "act", "pool", "sp"])
    wnat_u = bc[:, 0:2048]
    wnat_g = bc[:, 2048:4096]
    wT = lambda par, w: bc[:, 4096 + par * 4096 + w * 2048: 4096 + par * 4096 + (w + 1) * 2048]
    tu_f0, tu_f1 = ctx.scr_f(0), ctx.scr_f(2)
    tg_f0, tg_f1 = ctx.scr_f(1), ctx.scr_f(3)
    zin0, zin1 = ctx.scr_b(0), ctx.scr_b(2)
    zev0, zev1 = ctx.scr_b(1), ctx.scr_b(3)
    prev_tr = {0: [], 1: []}
    prev_nat = []
    prev_zu = {0: [], 1: []}
    prev_zz = {0: [], 1: []}
    prev_ze = {0: [], 1: []}
    for fi in range(NCI):
        par = fi % 2
        ldn = ctx.dma_group("pool", [
            (wnat_u, aps["W_up"][fi * 128:(fi + 1) * 128, :]),
            (wnat_g, aps["W_gate"][fi * 128:(fi + 1) * 128, :])], deps=prev_nat)
        trt = []
        for w in range(2):
            wn = (wnat_u, wnat_g)[w]
            trt.append(ctx.dma_group(("sp", "act")[w], [
                (wT(par, w)[:, d_ * 128:(d_ + 1) * 128], wn[:, d_ * 128:(d_ + 1) * 128])
                for d_ in range(16)], deps=[ldn] + prev_tr[par], transpose=True))
        prev_nat = trt
        prev_tr[par] = []
        for tt in range(NTT):
            sl = slice(tt * TT, (tt + 1) * TT)
            q = (fi * NTT + tt) % 2
            bu = ("w0", "w2")[q]
            bg = ("w1", "w3")[q]
            psu = pp.get(bu)
            psg = pp.get(bg)
            tu_f = (tu_f0, tu_f1)[q]
            tg_f = (tg_f0, tg_f1)[q]
            zin = (zin0, zin1)[q]
            zev = (zev0, zev1)[q]
            mmu = mmg = None
            for d_ in range(16):
                mmu = ctx.emit("pe", lambda t, o=psu[:], l=wT(par, 0)[:, d_ * 128:(d_ + 1) * 128],
                               r=x1_c(d_)[:, sl], st=(d_ == 0), sp=(d_ == 15):
                               t.matmul(o, l, r, start=st, stop=sp),
                               deps=[trt[0]] + (pp.deps(bu) if d_ == 0 else []),
                               token=(d_ == 15))
            for d_ in range(16):
                mmg = ctx.emit("pe", lambda t, o=psg[:], l=wT(par, 1)[:, d_ * 128:(d_ + 1) * 128],
                               r=x1_c(d_)[:, sl], st=(d_ == 0), sp=(d_ == 15):
                               t.matmul(o, l, r, start=st, stop=sp),
                               deps=[trt[1]] + (pp.deps(bg) if d_ == 0 else []),
                               token=(d_ == 15))
            prev_tr[par].extend([mmu, mmg])
            t_u = ctx.emit("dve", lambda v, s=su2[:, fi:fi + 1], o=tu_f[:], p=psu[:]:
                           v.tensor_scalar(o, p, s, None, ALU.mult),
                           deps=[mmu, tv2] + prev_zu[q], token=True)
            pp.consumed(bu, t_u)
            t_g = ctx.emit("act", lambda a, s=sg[:, fi:fi + 1], o=tg_f[:], p=psg[:]:
                           a.activation(o, p, AF.Silu, scale=s),
                           deps=[mmg, tv3] + prev_zu[q], token=True)
            pp.consumed(bg, t_g)
            t_z = ctx.emit("dve", lambda v, o=zin[:], a_=tu_f[:], b_=tg_f[:]:
                           v.tensor_mul(o, a_, b_),
                           deps=[t_u, t_g] + prev_zz[q], token=True)
            psz = pp.get("b0")
            mmz = ctx.emit("pe", lambda t, i=zin[:]: t.matmul(psz[:], h128[:], i, start=True, stop=True),
                           deps=[t_z, th] + pp.deps("b0"), token=True)
            t_ze = ctx.emit("act", lambda a, o=zev[:]: a.copy(o, psz[:]), deps=[mmz] + prev_ze[q], token=True)
            pp.consumed("b0", t_ze)
            tzo = ctx.dma("pool", za_hbm[tt][:, fi * TT:(fi + 1) * TT], zev[:], deps=[t_ze])
            prev_zu[q] = [t_z]
            prev_zz[q] = [mmz]
            prev_ze[q] = [tzo]

    ctx.barrier(["pe", "dve", "act", "pool", "sp"])
    wd2b = lambda par: bc[:, par * 8192:(par + 1) * 8192]
    prev_wd = {0: [], 1: []}
    prev_za = []
    for tt in range(NTT):
        sl = slice(tt * TT, (tt + 1) * TT)
        lza = ctx.dma("pool", bb[:, 0:NCI * TT], za_hbm[tt], deps=prev_za)
        for f2 in range(NC):
            par = f2 % 2
            ldw = ctx.dma("pool", wd2b(par), wd2_hbm[f2], deps=prev_wd[par])
            prev_wd[par] = []
            bank = ("w0", "w1")[par]
            ps = pp.get(bank)
            mmd = None
            for c in range(NCI):
                mmd = ctx.emit("pe", lambda t, o=ps[:], l=wd2b(par)[:, c * 128:(c + 1) * 128],
                               r=za_c(c),
                               st=(c == 0), sp=(c == NCI - 1):
                               t.matmul(o, l, r, start=st, stop=sp),
                               deps=[ldw, lza] + (pp.deps(bank) if c == 0 else []),
                               token=(c == NCI - 1))
            prev_wd[par].append(mmd)
            prev_za.append(mmd)
            tev = ctx.emit("dve", lambda v, o=x_c(f2)[:, sl], s=sdp[:, f2:f2 + 1], p=ps[:]:
                           v.scalar_tensor_tensor(o, p, s, o, ALU.mult, ALU.add),
                           deps=[mmd, tv4], token=True)
            pp.consumed(bank, tev)

    ctx.barrier(["pe", "dve", "act"])
    emit_ln(ctx, consts, x_c, x_c)


def np_h128():
    h = np.array([[1.0]], dtype=np.float32)
    while h.shape[0] < 128:
        h = np.block([[h, h], [h, -h]])
    return h


def run_l0_ffn(xin_list, inputs, n_cores=8):
    nc = build_l0_ffn()
    ident, _ = np_consts()
    h128 = np_h128()
    in_maps = []
    for b in range(n_cores):
        in_maps.append({
            "xin": xin_list[b],
            "W_up": inputs["L0_ffn_W_up"], "W_gate": inputs["L0_ffn_W_gate"],
            "W_down": inputs["L0_ffn_W_down"],
            "SU_ug": inputs["L0_ffn_SU_ug"], "SU_dp": inputs["L0_ffn_SU_dp"],
            "Wscale_ug": inputs["L0_ffn_Wscale_ug"], "Wscale_dp": inputs["L0_ffn_Wscale_dp"],
            "c_ident": ident, "c_h128": h128,
        })
    res = run_bass_kernel_spmd(nc, in_maps, core_ids=list(range(n_cores)))
    return [r["xout"] for r in res.results]



H, HD, M, CHUNK = 16, 128, 4, 128
NCH = S // CHUNK
SCALE_L1 = HD ** (-0.5)
EPS_L1 = 1e-6


def build_l1_attn():
    nc = bass.Bass("TRN2", target_bir_lowering=False, debug=False)
    aps = {}
    aps["memories"] = nc.dram_tensor("memories", [M, H, HD, HD], F32, kind="ExternalInput").ap()
    aps["memory_norms"] = nc.dram_tensor("memory_norms", [M, H, HD], F32, kind="ExternalInput").ap()
    aps["w_q"] = nc.dram_tensor("w_q", [D, D], F32, kind="ExternalInput").ap()
    aps["w_k"] = nc.dram_tensor("w_k", [D, D], F32, kind="ExternalInput").ap()
    aps["w_v"] = nc.dram_tensor("w_v", [D, D], F32, kind="ExternalInput").ap()
    aps["w_o"] = nc.dram_tensor("w_o", [D, D], F32, kind="ExternalInput").ap()
    aps["gate"] = nc.dram_tensor("gate", [H], F32, kind="ExternalInput").ap()
    aps["c_ident"] = nc.dram_tensor("c_ident", [128, 128], F32, kind="ExternalInput").ap()
    aps["c_maskT"] = nc.dram_tensor("c_maskT", [128, 128], F32, kind="ExternalInput").ap()
    xin = nc.dram_tensor("xin", [NC, 128, S], BF16, kind="ExternalInput").ap()
    xout = nc.dram_tensor("xout", [NC, 128, S], BF16, kind="ExternalOutput").ap()
    dbg = None
    if os.environ.get("BK_DBG", "0") == "1":
        dbg = {"ao": nc.dram_tensor("d_ao", [NC, 128, S], BF16, kind="ExternalOutput").ap(),
               "sg": nc.dram_tensor("d_sg", [128, 20480], BF16, kind="ExternalOutput").ap()}
    es = ExitStack()
    with es:
        ctx = make_ctx(nc, es)
        consts = Consts(ctx, aps)
        x = ctx.sbuf("big_a", [128, NC * S], BF16)
        x_c = lambda c: x[:, c * S:(c + 1) * S]
        ctx.dma_group("pool", [(x_c(c), xin[c]) for c in range(NC)])
        ctx.barrier()
        emit_l1_attn(ctx, consts, aps, dbg)
        ctx.barrier(["pool"])
        ctx.dma_group("pool", [(xout[c], x_c(c)) for c in range(NC)])
        ctx.barrier(["sp"])
        ctx.build()
    return nc


def emit_l1_attn(ctx, consts, aps, dbg=None):
    nc = ctx.nc
    pp = ctx.psum_pool
    x = ctx.sbuf("big_a", [128, NC * S], BF16)
    x_c = lambda c: x[:, c * S:(c + 1) * S]
    ao = ctx.sbuf("big_b", [128, NC * S], BF16)
    ao_c = lambda c: ao[:, c * S:(c + 1) * S]
    bc = ctx.sbuf("big_c", [128, 20480], BF16)
    wT = lambda par, w: bc[:, w * 2048:(w + 1) * 2048]
    sgq_p = lambda p_: bc[:, 6144 + p_ * 6144: 6144 + p_ * 6144 + 2048]
    sgk_p = lambda p_: bc[:, 8192 + p_ * 6144: 8192 + p_ * 6144 + 2048]
    v_tm_p = lambda p_, c: bc[:, 10240 + p_ * 6144 + c * 128: 10240 + p_ * 6144 + (c + 1) * 128]
    wnat = bc[:, 18432:20480]
    maskT = ctx.sbuf("c_maskT_bf", [128, 128], BF16)
    tmk = ctx.dma("pool", maskT[:], aps["c_maskT"])

    mn_nat = bc[0:M * H, 18432:18944].bitcast(F32)[:, 0:HD]
    mnT = ctx.sbuf("mnT", [128, M * H], BF16)
    t0 = ctx.dma("pool", mn_nat, aps["memory_norms"].rearrange("m h d -> (m h) d"))
    ps1 = pp.get("s1")
    tp0 = ctx.emit("pe", lambda t: t.transpose(ps1[0:128, 0:M * H], mn_nat,
                                               consts.ident_f[0:M * H, 0:M * H]),
                   deps=[t0] + consts.ready + pp.deps("s1"), token=True)
    tmn = ctx.emit("dve", lambda v: v.tensor_copy(mnT[:], ps1[0:128, 0:M * H]), deps=[tp0], token=True)
    pp.consumed("s1", tmn)
    grow = ctx.sbuf("grow", [1, 4 * H], F32)
    g_sig = grow[:, 0:H]
    g1m = grow[:, H:2 * H]
    w5 = grow[:, 2 * H:2 * H + 8]
    graw = grow[:, 3 * H:4 * H]
    tg0 = ctx.dma("pool", graw, aps["gate"].rearrange("(a h) -> a h", a=1))
    tg1 = ctx.emit("act", lambda a: a.activation(g_sig, graw, AF.Sigmoid), deps=[tg0], token=True)
    tg2 = ctx.emit("dve", lambda v: v.tensor_scalar(g1m, g_sig, -1.0, 1.0, ALU.mult, ALU.add),
                   deps=[tg1], token=True)

    mem4 = ctx.scr_b(1)
    qsum = ctx.scr_f(4)
    qsum_bf = ctx.scr_b(6)
    S_f = ctx.scr_f(5)
    S_bf = ctx.scr_b(7)
    elu_t = ctx.scr_f(0)
    elu_e = ctx.scr_f(1)
    elu_r = ctx.scr_f(2)
    wbc5 = ctx.scr_f(3)
    scT = ctx.scr_b(0)
    kp_t = ctx.scr_b(2)
    out_b = ctx.scr_b(3)
    mem_acc = qsum[:, 128:256]
    rn4 = qsum[:, 256:260]
    rd1 = qsum[:, 260:261]

    prev_par = {0: [], 1: []}
    prev_nat = [tmn, tp0]
    ao_done = []
    prev_sg = {0: [], 1: []}
    prev_mem4 = []
    prev_row = []
    prev_state = []
    prev_chunk_dve = []
    prev_outb = []
    for h in range(H):
        par = h % 2
        sgq = sgq_p(par)
        sgk = sgk_p(par)
        v_tm = lambda c, p_=par: v_tm_p(p_, c)
        trt = []
        for w, wap in enumerate((aps["w_q"], aps["w_k"], aps["w_v"])):
            ldn = ctx.dma("pool", wnat, wap[h * 128:(h + 1) * 128, :],
                          deps=prev_nat + prev_par[0])
            tr = ctx.dma_group(("sp", "act", "sp")[w],
                               [(wT(0, w)[:, d_ * 128:(d_ + 1) * 128],
                                 wnat[:, d_ * 128:(d_ + 1) * 128]) for d_ in range(16)],
                               deps=[ldn], transpose=True)
            trt.append(tr)
            prev_nat = [tr]
        prev_par[0] = []
        ldm = ctx.dma("pool", mem4[:].rearrange("p (m e) -> p m e", e=128),
                      aps["memories"].rearrange("m h d e -> h d m e")[h], deps=prev_mem4)

        tqs0 = ctx.emit("dve", lambda v: v.memset(qsum[:, 0:1], 0.0), deps=prev_row, token=True)
        proj_done = []
        elu_prev = []
        for which, (wi, dst) in enumerate(((0, sgq), (1, sgk))):
            for ss in range(NTT):
                sl = slice(ss * TT, (ss + 1) * TT)
                bank = ("w0", "w1")[ss % 2]
                psq = pp.get(bank)
                mmq = None
                for d_ in range(16):
                    mmq = ctx.emit("pe", lambda t, o=psq[:], l=wT(par, wi)[:, d_ * 128:(d_ + 1) * 128],
                                   r=x_c(d_)[:, sl], st=(d_ == 0), sp=(d_ == 15):
                                   t.matmul(o, l, r, start=st, stop=sp),
                                   deps=[trt[wi]] + (pp.deps(bank) if d_ == 0 else []) + prev_sg[par],
                                   token=(d_ == 15))
                if which == 0:
                    tq1 = ctx.emit("dve", lambda v, p=psq[:]: v.reduce_sum(elu_r[:, 0:1], p, axis=AX),
                                   deps=[mmq, tqs0] + elu_prev, token=True)
                    tq2 = ctx.emit("dve", lambda v: v.tensor_add(qsum[:, 0:1], qsum[:, 0:1], elu_r[:, 0:1]),
                                   deps=[tq1], token=True)
                    proj_done.append(tq2)
                e1 = ctx.emit("dve", lambda v, p=psq[:]: v.tensor_scalar(elu_t[:], p, 0.0, None, ALU.min),
                              deps=[mmq] + elu_prev, token=True)
                e2 = ctx.emit("act", lambda a: a.activation(elu_e[:], elu_t[:], AF.Exp),
                              deps=[e1], token=True)
                e3 = ctx.emit("dve", lambda v, p=psq[:]: v.tensor_scalar(elu_r[:], p, 0.0, None, ALU.max),
                              deps=[e1] + ([tq2] if which == 0 else []) + elu_prev, token=True)
                e4 = ctx.emit("dve", lambda v, o=dst[:, sl]: v.tensor_add(o, elu_r[:], elu_e[:]),
                              deps=[e2, e3], token=True)
                pp.consumed(bank, e4)
                proj_done.append(e4)
                elu_prev = [e4]
        for c in range(NCH):
            bank = ("w0", "w1")[c % 2]
            psv = pp.get(bank)
            mmv = None
            for d_ in range(16):
                mmv = ctx.emit("pe", lambda t, o=psv[:, 0:128], l=x_c(d_)[:, c * 128:(c + 1) * 128],
                               r=wT(par, 2)[:, d_ * 128:(d_ + 1) * 128], st=(d_ == 0), sp=(d_ == 15):
                               t.matmul(o, l, r, start=st, stop=sp),
                               deps=[trt[2]] + (pp.deps(bank) if d_ == 0 else []) + prev_sg[par],
                               token=(d_ == 15))
            tve = ctx.emit("dve", lambda v, o=v_tm(c), p=psv[:, 0:128]: v.tensor_copy(o, p),
                           deps=[mmv], token=True)
            pp.consumed(bank, tve)
            proj_done.append(tve)
        tqb = ctx.emit("dve", lambda v: v.tensor_copy(qsum_bf[:, 0:1], qsum[:, 0:1]),
                       deps=proj_done[-1:] + [tq2], token=True)
        psr = pp.get("s2")
        mmr = ctx.emit("pe", lambda t, r=mnT[:, h::H]: t.matmul(psr[0:1, 0:M], qsum_bf[:, 0:1], r,
                                                start=True, stop=True),
                       deps=[tqb, tmn] + pp.deps("s2"), token=True)
        r0 = wbc5[0:1, 8:12]
        r1 = wbc5[0:1, 12:13]
        tr0 = ctx.emit("dve", lambda v: v.tensor_scalar(r0, psr[0:1, 0:M], SCALE_L1 / S, None, ALU.mult),
                       deps=[mmr] + prev_row, token=True)
        pp.consumed("s2", tr0)
        tr1 = ctx.emit("dve", lambda v: v.reduce_max(r1, r0, axis=AX), deps=[tr0], token=True)
        tr2 = ctx.emit("dve", lambda v: v.tensor_scalar(r0, r0, r1, None, ALU.subtract), deps=[tr1], token=True)
        tr3 = ctx.emit("act", lambda a: a.activation(r0, r0, AF.Exp), deps=[tr2], token=True)
        tr4 = ctx.emit("dve", lambda v: v.reduce_sum(r1, r0, axis=AX), deps=[tr3], token=True)
        tr5 = ctx.emit("dve", lambda v: v.reciprocal(r1, r1), deps=[tr4], token=True)
        tr6 = ctx.emit("dve", lambda v, g=g_sig[0:1, h:h + 1]: v.tensor_scalar(w5[:, 0:4], r0, r1, g,
                                                        ALU.mult, ALU.mult), deps=[tr5, tg2], token=True)
        tr7 = ctx.emit("dve", lambda v, g=g1m[0:1, h:h + 1]: v.tensor_copy(w5[:, 4:5], g), deps=[tr6], token=True)
        psb = pp.get("s2")
        mmb = ctx.emit("pe", lambda t: t.matmul(psb[0:128, 0:5], consts.ones_row_f, w5[:, 0:5],
                                                start=True, stop=True),
                       deps=[tr7] + pp.deps("s2"), token=True)
        tbe = ctx.emit("dve", lambda v: v.tensor_copy(wbc5[:, 0:5], psb[0:128, 0:5]),
                       deps=[mmb] + prev_row, token=True)
        pp.consumed("s2", tbe)
        prev_row = [tbe]

        ts0 = ctx.emit("dve", lambda v: v.memset(S_f[:, 0:129], 0.0), deps=prev_state, token=True)
        ts1 = ctx.emit("dve", lambda v: v.memset(S_bf[:, 0:129], 0.0), deps=prev_state, token=True)

        for c in range(NCH):
            csl = slice(c * 128, (c + 1) * 128)
            psm = pp.get("b0")
            mm_m = ctx.emit("pe", lambda t, l=sgq[:, csl]: t.matmul(psm[:], l, mem4[:], start=True, stop=True),
                            deps=[ldm, proj_done[-1]] + pp.deps("b0"), token=True)
            psn = pp.get("s1")
            mm_n = ctx.emit("pe", lambda t, l=sgq[:, csl], r=mnT[:, h::H]: t.matmul(psn[0:128, 0:M], l, r,
                                                                    start=True, stop=True),
                            deps=[tmn] + pp.deps("s1"), token=True)
            tn1 = ctx.emit("dve", lambda v: v.tensor_scalar(rn4[:], psn[0:128, 0:M], EPS_L1, None, ALU.max),
                           deps=[mm_n] + prev_chunk_dve, token=True)
            pp.consumed("s1", tn1)
            tn2 = ctx.emit("dve", lambda v: v.reciprocal(rn4[:], rn4[:]), deps=[tn1], token=True)
            tn3 = ctx.emit("dve", lambda v: v.tensor_mul(rn4[:], rn4[:], wbc5[:, 0:4]),
                           deps=[tn2, tbe], token=True)
            tm0 = ctx.emit("dve", lambda v: v.tensor_scalar(mem_acc[:], psm[:, 0:128], rn4[:, 0:1], None, ALU.mult),
                           deps=[mm_m, tn3], token=True)
            tmx = tm0
            for m in range(1, M):
                tmx = ctx.emit("dve", lambda v, mm=m: v.scalar_tensor_tensor(
                    mem_acc[:], psm[:, mm * 128:(mm + 1) * 128], rn4[:, mm:mm + 1], mem_acc[:],
                    ALU.mult, ALU.add), deps=[tmx], token=True)
            pp.consumed("b0", tmx)
            pss = pp.get("w2")
            mm_s = ctx.emit("pe", lambda t, l=sgk[:, csl], r=sgq[:, csl]:
                            t.matmul(pss[:, 0:128], l, r, start=True, stop=True),
                            deps=pp.deps("w2") + [proj_done[-1]], token=True)
            tms = ctx.emit("dve", lambda v: v.tensor_mul(scT[:, 0:128], pss[:, 0:128], maskT[:]),
                           deps=[mm_s, tmk] + prev_chunk_dve, token=True)
            pp.consumed("w2", tms)
            pst = pp.get("tp")
            mm_t = ctx.emit("pe", lambda t, i=sgk[:, csl]: t.transpose(pst[:, 0:128], i, consts.ident_bf[:]),
                            deps=pp.deps("tp") + [proj_done[-1]], token=True)
            tkt = ctx.emit("act", lambda a: a.copy(kp_t[:, 0:128], pst[:, 0:128]),
                           deps=[mm_t] + prev_outb, token=True)
            pp.consumed("tp", tkt)
            psu = pp.get("w3")
            mm_i = None
            if c > 0:
                mm_i = ctx.emit("pe", lambda t, l=sgq[:, csl]: t.matmul(psu[:, 0:128], l, S_bf[:, 0:128],
                                                                        start=True, stop=False),
                                deps=pp.deps("w3") + prev_state, token=True)
            mm_a = ctx.emit("pe", lambda t, r=v_tm(c), st=(c == 0): t.matmul(psu[:, 0:128], scT[:, 0:128], r,
                                                     start=st, stop=True),
                            deps=[tms] + ([mm_i] if mm_i else pp.deps("w3")), token=True)
            psd = pp.get("s1")
            mm_d1 = None
            if c > 0:
                mm_d1 = ctx.emit("pe", lambda t, l=sgq[:, csl]: t.matmul(psd[:, 4:5], l, S_bf[:, 128:129],
                                                                         start=True, stop=False),
                                 deps=pp.deps("s1") + prev_state, token=True)
            mm_d2 = ctx.emit("pe", lambda t, st=(c == 0): t.matmul(psd[:, 4:5], scT[:, 0:128], consts.ones_col_bf[:],
                                                      start=st, stop=True),
                             deps=[tms] + ([mm_d1] if mm_d1 else pp.deps("s1")) + consts.ready, token=True)
            td1 = ctx.emit("dve", lambda v: v.tensor_scalar(rd1[:], psd[:, 4:5], EPS_L1, None, ALU.max),
                           deps=[mm_d2] + prev_chunk_dve, token=True)
            pp.consumed("s1", td1)
            td2 = ctx.emit("dve", lambda v: v.reciprocal(rd1[:], rd1[:]), deps=[td1], token=True)
            td3 = ctx.emit("dve", lambda v: v.tensor_mul(rd1[:], rd1[:], wbc5[:, 4:5]),
                           deps=[td2, tbe], token=True)
            tob = ctx.emit("dve", lambda v: v.scalar_tensor_tensor(out_b[:, 0:128], psu[:, 0:128],
                                                                   rd1[:, 0:1], mem_acc[:],
                                                                   ALU.mult, ALU.add),
                           deps=[mm_a, td3, tmx], token=True)
            pp.consumed("w3", tob)
            mm_o = ctx.emit("pe", lambda t: t.transpose(pst[:, 128:256], out_b[:, 0:128], consts.ident_bf[:]),
                            deps=[tob] + pp.deps("tp"), token=True)
            toe = ctx.emit("act", lambda a, o=ao_c(h)[:, csl]: a.copy(o, pst[:, 128:256]),
                           deps=[mm_o], token=True)
            pp.consumed("tp", toe)
            ao_done.append(toe)
            prev_outb = [toe]
            psk = pp.get("s2")
            mm_k = ctx.emit("pe", lambda t, r=v_tm(c): t.matmul(psk[:, 0:128], kp_t[:, 0:128], r,
                                                     start=True, stop=True),
                            deps=[tkt] + pp.deps("s2"), token=True)
            tz1 = ctx.emit("dve", lambda v, i=sgk[:, csl]: v.reduce_sum(elu_r[:, 0:1], i, axis=AX),
                           deps=[proj_done[-1]] + prev_state + prev_chunk_dve, token=True)
            ts_a = ctx.emit("dve", lambda v: v.tensor_add(S_f[:, 0:128], S_f[:, 0:128], psk[:, 0:128]),
                            deps=[mm_k] + ([mm_i] if mm_i else []) + ([mm_d1] if mm_d1 else []), token=True)
            pp.consumed("s2", ts_a)
            ts_z = ctx.emit("dve", lambda v: v.tensor_add(S_f[:, 128:129], S_f[:, 128:129], elu_r[:, 0:1]),
                            deps=[tz1], token=True)
            ts_c = ctx.emit("dve", lambda v: v.tensor_copy(S_bf[:, 0:129], S_f[:, 0:129]),
                            deps=[ts_a, ts_z], token=True)
            prev_state = [ts_c]
            prev_chunk_dve = [ts_c]
        prev_sg[par] = [ts_c, mm_k]
        prev_par[0] = [mmv]
        prev_mem4 = [tmx]

    if dbg is not None:
        ctx.barrier(["pool"])
        ao3 = ao.rearrange("p (c s) -> p c s", c=NC)
        for c in range(NC):
            ctx.dma("pool", dbg["ao"][c], ao3[:, c])
        ctx.dma("pool", dbg["sg"], bc[:])
    ctx.barrier(["pe", "dve", "act", "pool", "sp"])
    prev_nat2 = []
    prev_wo = {0: [], 1: []}
    for fo in range(NC):
        par = fo % 2
        ldn = ctx.dma("pool", wnat, aps["w_o"][fo * 128:(fo + 1) * 128, :], deps=prev_nat2)
        wot = (bc[:, 0:2048], bc[:, 2048:4096])[par]
        tr = ctx.dma_group("sp", [(wot[:, d_ * 128:(d_ + 1) * 128],
                                   wnat[:, d_ * 128:(d_ + 1) * 128]) for d_ in range(16)],
                           deps=[ldn] + prev_wo[par], transpose=True)
        prev_nat2 = [tr]
        prev_wo[par] = []
        for tt in range(NTT):
            sl = slice(tt * TT, (tt + 1) * TT)
            bank = ("w0", "w1")[tt % 2]
            pso = pp.get(bank)
            mmo = None
            for d_ in range(16):
                mmo = ctx.emit("pe", lambda t, o=pso[:], l=wot[:, d_ * 128:(d_ + 1) * 128],
                               r=ao_c(d_)[:, sl], st=(d_ == 0), sp=(d_ == 15):
                               t.matmul(o, l, r, start=st, stop=sp),
                               deps=[tr] + (pp.deps(bank) if d_ == 0 else []), token=(d_ == 15))
            prev_wo[par].append(mmo)
            tev = ctx.emit("dve", lambda v, o=x_c(fo)[:, sl], p=pso[:]: v.tensor_add(o, o, p),
                           deps=[mmo], token=True)
            pp.consumed(bank, tev)

    ctx.barrier(["pe", "dve", "act"])
    emit_ln(ctx, consts, x_c, x_c)


def np_maskT():
    j = np.arange(128)[:, None]
    i = np.arange(128)[None, :]
    return (i >= j).astype(np.float32)


def run_l1_attn(xin_list, inputs, n_cores=8):
    nc = build_l1_attn()
    ident, _ = np_consts()
    in_maps = []
    for b in range(n_cores):
        in_maps.append({
            "xin": xin_list[b],
            "memories": inputs["L1_attn_memories"], "memory_norms": inputs["L1_attn_memory_norms"],
            "w_q": inputs["L1_attn_w_q"], "w_k": inputs["L1_attn_w_k"],
            "w_v": inputs["L1_attn_w_v"], "w_o": inputs["L1_attn_w_o"],
            "gate": inputs["L1_attn_gate"],
            "c_ident": ident, "c_maskT": np_maskT(),
        })
    res = run_bass_kernel_spmd(nc, in_maps, core_ids=list(range(n_cores)))
    return [r["xout"] for r in res.results]



FFN = 1408
NEXP = 8
NCF = 11


def build_l1_moe():
    nc = bass.Bass("TRN2", target_bir_lowering=False, debug=False)
    aps = {}
    aps["wg"] = nc.dram_tensor("wg", [NEXP, D], F32, kind="ExternalInput").ap()
    aps["w1"] = nc.dram_tensor("w1", [NEXP, D, 2 * FFN], F32, kind="ExternalInput").ap()
    aps["w2"] = nc.dram_tensor("w2", [NEXP, FFN, D], F32, kind="ExternalInput").ap()
    aps["c_ident"] = nc.dram_tensor("c_ident", [128, 128], F32, kind="ExternalInput").ap()
    xin = nc.dram_tensor("xin", [NC, 128, S], BF16, kind="ExternalInput").ap()
    out = nc.dram_tensor("out", [S, D], F32, kind="ExternalOutput").ap()
    acc_hbm = nc.dram_tensor("acc_hbm", [NC, 128, S], F32, kind="Internal").ap()
    crow_hbm = nc.dram_tensor("crow_hbm", [NEXP, S], F32, kind="Internal").ap()
    es = ExitStack()
    with es:
        ctx = make_ctx(nc, es)
        consts = Consts(ctx, aps)
        x = ctx.sbuf("big_a", [128, NC * S], BF16)
        x_c = lambda c: x[:, c * S:(c + 1) * S]
        ctx.dma_group("pool", [(x_c(c), xin[c]) for c in range(NC)])
        ctx.barrier()
        emit_l1_moe(ctx, consts, aps, acc_hbm, crow_hbm)
        emit_final_out(ctx, consts, out, acc_hbm)
        ctx.barrier(["sp"])
        ctx.build()
    return nc


def emit_l1_moe(ctx, consts, aps, acc_hbm, crow_hbm):
    nc = ctx.nc
    pp = ctx.psum_pool
    x = ctx.sbuf("big_a", [128, NC * S], BF16)
    x_c = lambda c: x[:, c * S:(c + 1) * S]
    y = ctx.sbuf("big_b", [128, NC * S], BF16)
    y_c = lambda j, tt: y[:, j * S + tt * TT: j * S + (tt + 1) * TT]
    bc = ctx.sbuf("big_c", [128, 20480], BF16)
    wgT = bc[:, 8704:8832]
    ce_all = bc[:, 16384:20480].bitcast(F32)
    rows = ctx.sbuf("scr_row", [1, 3 * TT], F32)
    ce_row = rows[0:1, 0:TT]

    wg_nat = bc[0:8, 0:2 * D].bitcast(F32)
    t0 = ctx.dma("pool", wg_nat, aps["wg"])
    ps1 = pp.get("s1")
    wgt_done = []
    for c in range(NC):
        tp0 = ctx.emit("pe", lambda t, i=wg_nat[:, c * 128:(c + 1) * 128]:
                       t.transpose(ps1[0:128, 0:8], i, consts.ident_f[0:8, 0:8]),
                       deps=[t0] + consts.ready + pp.deps("s1"), token=True)
        te0 = ctx.emit("dve", lambda v, o=wgT[:, c * 8:(c + 1) * 8]:
                       v.tensor_copy(o, ps1[0:128, 0:8]), deps=[tp0], token=True)
        pp.consumed("s1", te0)
        wgt_done.append(te0)
    sf = ctx.scr_f(0)
    scores = sf[:, 0:8]
    mcol = sf[:, 8:9]
    m1 = sf[:, 9:10]
    m2 = sf[:, 10:11]
    dn = sf[:, 11:12]
    is1 = sf[:, 16:24]
    msk = sf[:, 24:32]
    Cc = sf[:, 32:40]
    stage8 = bc[0:8, 2 * D * 2:2 * D * 2 + 512].bitcast(F32)[:, 0:128]
    prev_g = []
    crow_toks = []
    for tc in range(NC):
        psl = pp.get("s2")
        mml = None
        for d_ in range(NC):
            mml = ctx.emit("pe", lambda t, o=psl[0:128, 0:8], l=x_c(d_)[:, tc * 128:(tc + 1) * 128],
                           r=wgT[:, d_ * 8:(d_ + 1) * 8], st=(d_ == 0), sp=(d_ == NC - 1):
                           t.matmul(o, l, r, start=st, stop=sp),
                           deps=[wgt_done[d_]] + (pp.deps("s2") if d_ == 0 else []),
                           token=(d_ == NC - 1))
        g1 = ctx.emit("dve", lambda v: v.tensor_copy(scores, psl[0:128, 0:8]), deps=[mml] + prev_g, token=True)
        pp.consumed("s2", g1)
        g2 = ctx.emit("dve", lambda v: v.reduce_max(mcol, scores, axis=AX), deps=[g1], token=True)
        g3 = ctx.emit("dve", lambda v: v.tensor_scalar(scores, scores, mcol, None, ALU.subtract),
                      deps=[g2], token=True)
        g4 = ctx.emit("act", lambda a: a.activation(scores, scores, AF.Exp), deps=[g3], token=True)
        g5 = ctx.emit("dve", lambda v: v.reduce_sum(mcol, scores, axis=AX), deps=[g4], token=True)
        g6 = ctx.emit("dve", lambda v: v.reciprocal(mcol, mcol), deps=[g5], token=True)
        g7 = ctx.emit("dve", lambda v: v.tensor_scalar(scores, scores, mcol, None, ALU.mult),
                      deps=[g6], token=True)
        g8 = ctx.emit("dve", lambda v: v.reduce_max(m1, scores, axis=AX), deps=[g7], token=True)
        g9 = ctx.emit("dve", lambda v: v.tensor_scalar(is1, scores, m1, None, ALU.is_ge), deps=[g8], token=True)
        g10 = ctx.emit("dve", lambda v: v.scalar_tensor_tensor(msk, is1, -1e30, scores, ALU.mult, ALU.add),
                       deps=[g9], token=True)
        g11 = ctx.emit("dve", lambda v: v.reduce_max(m2, msk, axis=AX), deps=[g10], token=True)
        g12 = ctx.emit("dve", lambda v: v.tensor_scalar(is1, scores, m2, None, ALU.is_ge), deps=[g11, g10], token=True)
        g13 = ctx.emit("dve", lambda v: v.tensor_add(dn, m1, m2), deps=[g11], token=True)
        g14 = ctx.emit("dve", lambda v: v.tensor_scalar(dn, dn, 1e-20, None, ALU.add), deps=[g13], token=True)
        g15 = ctx.emit("dve", lambda v: v.reciprocal(dn, dn), deps=[g14], token=True)
        g16 = ctx.emit("dve", lambda v: v.tensor_mul(Cc, scores, is1), deps=[g12, g15], token=True)
        g17 = ctx.emit("dve", lambda v: v.tensor_scalar(Cc, Cc, dn, None, ALU.mult), deps=[g16], token=True)
        pst = pp.get("s1")
        g19 = ctx.emit("pe", lambda t: t.transpose(pst[0:8, 0:128], Cc, consts.ident_f[:]),
                       deps=[g17] + pp.deps("s1"), token=True)
        g20 = ctx.emit("dve", lambda v: v.tensor_copy(stage8, pst[0:8, 0:128]), deps=[g19] + prev_g, token=True)
        pp.consumed("s1", g20)
        tdc = ctx.dma("pool", crow_hbm[:, tc * 128:(tc + 1) * 128], stage8, deps=[g20])
        prev_g = [tdc]
        crow_toks.append(tdc)
    prev_ce = list(crow_toks)

    blkA = lambda d_: bc[:, d_ * 512:(d_ + 1) * 512]
    blkB = lambda d_: bc[:, 8192 + d_ * 512:8192 + (d_ + 1) * 512]
    w2blk = lambda c: bc[:, c * 512:(c + 1) * 512]
    sa_f0, sa_f1 = ctx.scr_f(1), ctx.scr_f(3)
    ty_f0, ty_f1 = ctx.scr_f(2), ctx.scr_f(4)
    ev0 = ctx.scr_f(0)
    ev1 = ctx.scr_f(5)
    prev_blk = [mml]
    prev_w2 = []
    prev_y1 = {0: [], 1: []}
    prev_y2 = {0: [], 1: []}
    prev_acc = {0: [], 1: []}
    acc_order = []
    for e in range(NEXP):
        ce_done = []
        for tt in range(NTT):
            tcr = ctx.dma("pool", ce_row, crow_hbm[e, tt * TT:(tt + 1) * TT].rearrange("(a t) -> a t", a=1),
                          deps=prev_ce)
            psb = pp.get("b0")
            mcb = ctx.emit("pe", lambda t: t.matmul(psb[:], consts.ones_row_f, ce_row,
                                                    start=True, stop=True),
                           deps=[tcr] + pp.deps("b0"), token=True)
            tce = ctx.emit("dve", lambda v, o=ce_all[:, tt * TT:(tt + 1) * TT]:
                           v.tensor_copy(o, psb[:]), deps=[mcb] + prev_ce, token=True)
            pp.consumed("b0", tce)
            ce_done.append(tce)
            prev_ce = [tce]
        up_done = []
        for wp in range(3):
            wa = 512 if wp < 2 else 384
            nf = wa // 128
            lda = ctx.dma_group("pool", [
                (blkA(d_)[:, 0:wa], aps["w1"][e, d_ * 128:(d_ + 1) * 128, wp * 512:wp * 512 + wa])
                for d_ in range(NC)], deps=prev_blk)
            ldb = ctx.dma_group("pool", [
                (blkB(d_)[:, 0:wa], aps["w1"][e, d_ * 128:(d_ + 1) * 128, FFN + wp * 512:FFN + wp * 512 + wa])
                for d_ in range(NC)])
            prev_blk = []
            for f1 in range(nf):
                j = wp * 4 + f1
                for tt in range(NTT):
                    sl = slice(tt * TT, (tt + 1) * TT)
                    psA = pp.get("w0")
                    psB = pp.get("w1")
                    mma = mmb = None
                    for d_ in range(NC):
                        mma = ctx.emit("pe", lambda t, o=psA[:], l=blkA(d_)[:, f1 * 128:(f1 + 1) * 128],
                                       r=x_c(d_)[:, sl], st=(d_ == 0), sp=(d_ == NC - 1):
                                       t.matmul(o, l, r, start=st, stop=sp),
                                       deps=[lda] + (pp.deps("w0") if d_ == 0 else []),
                                       token=(d_ == NC - 1))
                    for d_ in range(NC):
                        mmb = ctx.emit("pe", lambda t, o=psB[:], l=blkB(d_)[:, f1 * 128:(f1 + 1) * 128],
                                       r=x_c(d_)[:, sl], st=(d_ == 0), sp=(d_ == NC - 1):
                                       t.matmul(o, l, r, start=st, stop=sp),
                                       deps=[ldb] + (pp.deps("w1") if d_ == 0 else []),
                                       token=(d_ == NC - 1))
                    prev_blk.extend([mma, mmb])
                    q = (j * NTT + tt) % 2
                    sa_f = (sa_f0, sa_f1)[q]
                    ty_f = (ty_f0, ty_f1)[q]
                    tsa = ctx.emit("act", lambda a, p=psA[:], o=sa_f[:]: a.activation(o, p, AF.Silu),
                                   deps=[mma] + prev_y1[q], token=True)
                    pp.consumed("w0", tsa)
                    tty = ctx.emit("dve", lambda v, p=psB[:], o=ty_f[:], i=sa_f[:]: v.tensor_mul(o, i, p),
                                   deps=[tsa, mmb] + prev_y2[q], token=True)
                    pp.consumed("w1", tty)
                    tyb = ctx.emit("dve", lambda v, o=y_c(j, tt), cb=ce_all[:, sl], i=ty_f[:]:
                                   v.tensor_mul(o, i, cb),
                                   deps=[tty, ce_done[tt]], token=True)
                    prev_y1[q] = [tty]
                    prev_y2[q] = [tyb]
                    up_done.append(tyb)
        for fw in range(4):
            ldw = ctx.dma_group("pool", [
                (w2blk(c), aps["w2"][e, c * 128:(c + 1) * 128, fw * 512:(fw + 1) * 512])
                for c in range(NCF)], deps=prev_blk + prev_w2)
            prev_blk = []
            prev_w2 = []
            for f1 in range(4):
                f2 = fw * 4 + f1
                for tt in range(NTT):
                    sl = slice(tt * TT, (tt + 1) * TT)
                    par = (f1 * NTT + tt) % 2
                    bank = ("w0", "w1")[par]
                    psd = pp.get(bank)
                    mmd = None
                    for c in range(NCF):
                        mmd = ctx.emit("pe", lambda t, o=psd[:], l=w2blk(c)[:, f1 * 128:(f1 + 1) * 128],
                                       r=y_c(c, tt), st=(c == 0), sp=(c == NCF - 1):
                                       t.matmul(o, l, r, start=st, stop=sp),
                                       deps=[ldw, up_done[-1]] + (pp.deps(bank) if c == 0 else []),
                                       token=(c == NCF - 1))
                    prev_w2.append(mmd)
                    ev = (ev0, ev1)[par]
                    tev = ctx.emit("dve", lambda v, o=ev[:], p=psd[:]: v.tensor_copy(o, p),
                                   deps=[mmd] + prev_acc[par], token=True)
                    pp.consumed(bank, tev)
                    tacc = ctx.dma("pool", acc_hbm[f2][:, sl], ev[:], deps=[tev], accum=(e > 0))
                    prev_acc[par] = [tacc]
                    acc_order.append(tacc)
    return acc_order


def emit_final_out(ctx, consts, out_ap, acc_hbm):
    pp = ctx.psum_pool
    x = ctx.sbuf("big_a", [128, NC * S], BF16)
    x_c = lambda c: x[:, c * S:(c + 1) * S]
    ctx.barrier(["pe", "dve", "act", "pool", "sp"])
    ev0 = ctx.scr_f(1)
    ev1 = ctx.scr_f(2)
    prev_ld = {0: [], 1: []}
    for f2 in range(NC):
        for tt in range(NTT):
            sl = slice(tt * TT, (tt + 1) * TT)
            par = tt % 2
            ev = (ev0, ev1)[par]
            tld = ctx.dma("pool", ev[:], acc_hbm[f2][:, sl], deps=prev_ld[par])
            tad = ctx.emit("dve", lambda v, o=x_c(f2)[:, sl], e_=ev[:]: v.tensor_add(o, o, e_),
                           deps=[tld], token=True)
            prev_ld[par] = [tad]
    ctx.barrier(["pe", "dve", "act"])
    emit_ln(ctx, consts, x_c, x_c)
    ctx.barrier(["pe", "act"])
    bc = ctx.sbuf("big_c", [128, 20480], BF16)
    stage = bc[:, 0:2 * D].bitcast(F32)
    prev_st = []
    for sc in range(16):
        ev_toks = []
        for half in range(2):
            tpb = pp.get("tp")
            ttp = None
            for k in range(8):
                dc = half * 8 + k
                ttp = ctx.emit("pe", lambda t, i=x_c(dc)[:, sc * 128:(sc + 1) * 128],
                               o=tpb[:, k * 128:(k + 1) * 128]:
                               t.transpose(o, i, consts.ident_bf[:]),
                               deps=consts.ready + (pp.deps("tp") if k == 0 else []),
                               token=(k == 7))
            tev = ctx.emit("act", lambda a,
                           o=stage[:, half * 1024:(half + 1) * 1024].rearrange("p (c f) -> p c f", f=128),
                           i=tpb[:].rearrange("p (c f) -> p c f", f=128):
                           a.copy(o, i),
                           deps=[ttp] + prev_st, token=True)
            pp.consumed("tp", tev)
            ev_toks.append(tev)
        prev_st = [ctx.dma("pool", out_ap[sc * 128:(sc + 1) * 128, :], stage[:], deps=ev_toks)]
    ctx.barrier(["sp"])


def run_l1_moe(xin_list, inputs, n_cores=8):
    nc = build_l1_moe()
    ident, _ = np_consts()
    in_maps = []
    for b in range(n_cores):
        in_maps.append({
            "xin": xin_list[b],
            "wg": inputs["L1_ffn_wg"], "w1": inputs["L1_ffn_w1"], "w2": inputs["L1_ffn_w2"],
            "c_ident": ident,
        })
    res = run_bass_kernel_spmd(nc, in_maps, core_ids=list(range(n_cores)))
    return [r["out"] for r in res.results]


# version 07
# speedup vs baseline: 1.0471x; 1.0471x over previous
import os
import numpy as np
import ml_dtypes
from contextlib import ExitStack
import concourse.bass as bass
import concourse.mybir as mybir
from concourse.bass_utils import run_bass_kernel_spmd

F32 = mybir.dt.float32
BF16 = mybir.dt.bfloat16
AF = mybir.ActivationFunctionType
ALU = mybir.AluOpType
AX = mybir.AxisListType.X

D = 2048
S = 2048
NC = 16
TT = 512
NTT = S // TT
EPS_LN = 1e-5

SERIAL = os.environ.get("BK_SERIAL", "0") == "1"

ENGINE_ATTR = {"pe": "tensor", "act": "scalar", "dve": "vector",
               "pool": "gpsimd", "sp": "sync"}


class Ctx:

    def __init__(self, nc, es):
        self.nc = nc
        self.es = es
        self.prog = {e: [] for e in ENGINE_ATTR}
        self.esem = {e: es.enter_context(nc.semaphore(f"es_{e}")) for e in ENGINE_ATTR}
        self.NDS = {"sp": 6, "pool": 18, "act": 4}
        self.dsem = {}
        self.dcnt = {}
        for q, n in self.NDS.items():
            for i in range(n):
                self.dsem[f"{q}{i}"] = es.enter_context(nc.semaphore(f"ds_{q}{i}"))
                self.dcnt[f"{q}{i}"] = 0
        self.didx = {q: 0 for q in self.NDS}
        self.ecnt = {e: 0 for e in ENGINE_ATTR}
        self.water = {e: {} for e in ENGINE_ATTR}
        self.key2sem = {f"e:{e}": self.esem[e] for e in ENGINE_ATTR}
        self.key2sem.update({f"d:{k}": s for k, s in self.dsem.items()})
        self.last_token = None
        self._sb = {}

    def scr_b(self, i):
        return self.sbuf(f"scr_b{i}", [128, 512], BF16)

    def scr_f(self, i):
        return self.sbuf(f"scr_f{i}", [128, 512], F32)

    def sbuf(self, name, shape, dtype):
        key = name
        if key not in self._sb:
            self._sb[key] = self.es.enter_context(
                self.nc.sbuf_tensor(name, shape, dtype))
        return self._sb[key]

    def _wait(self, cons, token):
        key, val = token
        w = self.water[cons].get(key, 0)
        if val > w and key.startswith("d:"):
            cur = self.dcnt[key[2:]]
            if val != cur:
                import sys
                print(f"WARN: stale dma token {key}@{val} (cur {cur})", file=sys.stderr)
        if val > w:
            self.prog[cons].append(("wait", self.key2sem[key], val))
            self.water[cons][key] = val

    def emit(self, e, fn, deps=(), token=False):
        if SERIAL and self.last_token is not None:
            deps = list(deps) + [self.last_token]
        for t in deps:
            if t is not None:
                self._wait(e, t)
        tok = None
        if token or SERIAL:
            self.ecnt[e] += 1
            tok = (f"e:{e}", self.ecnt[e])
            self.prog[e].append(("inst_inc", fn, self.esem[e]))
            self.last_token = tok
        else:
            self.prog[e].append(("inst", fn))
        return tok

    def dma(self, q, out_ap, in_ap, deps=(), transpose=False, accum=False):
        return self.dma_group(q, [(out_ap, in_ap)], deps=deps, transpose=transpose, accum=accum)

    def dma_group(self, q, items, deps=(), transpose=False, accum=False):
        if SERIAL and self.last_token is not None:
            deps = list(deps) + [self.last_token]
        for t in deps:
            if t is not None:
                self._wait(q, t)
        sk = f"{q}{self.didx[q]}"
        self.didx[q] = (self.didx[q] + 1) % self.NDS[q]
        kw = {"transpose": True} if transpose else {}
        if accum:
            kw["accum_op"] = ALU.add
        for out_ap, in_ap in items:
            self.dcnt[sk] += 16
            self.prog[q].append(
                ("dma", (lambda eng, o=out_ap, i=in_ap, k=kw: eng.dma_start(out=o, in_=i, **k)),
                 self.dsem[sk]))
        tok = (f"d:{sk}", self.dcnt[sk])
        self.last_token = tok
        return tok

    def fence(self, engine, tokens):
        for t in tokens:
            if t is not None:
                self._wait(engine, t)

    def barrier(self, engines=None):
        engines = engines or list(ENGINE_ATTR)
        toks = [(f"e:{e}", self.ecnt[e]) for e in ENGINE_ATTR if self.ecnt[e] > 0]
        toks += [(f"d:{k}", v) for k, v in self.dcnt.items() if v > 0]
        for e in engines:
            for t in toks:
                if t[0] != f"e:{e}":
                    self._wait(e, t)

    def build(self):
        block = self.es.enter_context(self.nc.Block())

        def mk(e):
            def body(eng):
                for item in self.prog[e]:
                    kind = item[0]
                    if kind == "wait":
                        eng.wait_ge(item[1], item[2])
                    elif kind == "inst":
                        item[1](eng)
                    elif kind == "inst_inc":
                        item[1](eng).then_inc(item[2], 1)
                    elif kind == "dma":
                        item[1](eng).then_inc(item[2], 16)
            return body

        for e, attr in ENGINE_ATTR.items():
            getattr(block, attr)(mk(e))




def ceil_div(a, b):
    return (a + b - 1) // b


class Consts:

    def __init__(self, ctx, aps):
        self.ctx = ctx
        self.ident_bf = ctx.sbuf("c_ident_bf", [128, 128], BF16)
        self.ident_f = ctx.sbuf("c_ident_f", [128, 128], F32)
        self._ones = ctx.sbuf("c_ones_row_f", [65, 128], F32)
        self.ones_row_f = self._ones[0:1, :]
        self.ones_row64 = self._ones[64:65, :]
        self.ones_col_bf = ctx.sbuf("c_ones_col_bf", [128, 1], BF16)
        t1 = ctx.dma("pool", self.ident_bf[:], aps["c_ident"])
        t2 = ctx.dma("pool", self.ident_f[:], aps["c_ident"])
        self.ready = [t1, t2]
        self.ready.append(ctx.emit("dve", lambda v: v.memset(self._ones[:], 1.0), token=True))
        self.ready.append(ctx.emit("dve", lambda v: v.memset(self.ones_col_bf[:], 1.0), token=True))


def load_pervec(ctx, consts, name, hbm_ap, n, dtype=F32):
    nchunk = n // 128
    assert nchunk <= 128
    nat = ctx.sbuf(name + "_nat", [nchunk, 128], F32)
    out = ctx.sbuf(name, [128, nchunk], F32)
    t0 = ctx.dma("pool", nat[:], hbm_ap.rearrange("(c p) -> c p", p=128))
    ps = ctx.psum_pool.get("s1")
    tp = ctx.emit("pe", lambda t: t.transpose(ps[0:128, 0:nchunk], nat[:], consts.ident_f[0:nchunk, 0:nchunk]),
                  deps=[t0] + consts.ready + ctx.psum_pool.deps("s1"), token=True)
    te = ctx.emit("dve", lambda v: v.tensor_copy(out[:], ps[0:128, 0:nchunk]),
                  deps=[tp], token=True)
    ctx.psum_pool.consumed("s1", te)
    return out, te


class PsumPool:

    def __init__(self, ctx):
        self.ctx = ctx
        self.banks = {}
        self.cons_tok = {}

    def add(self, name, shape, dtype):
        self.banks[name] = self.ctx.es.enter_context(
            self.ctx.nc.psum_tensor(f"ps_{name}", shape, dtype))

    def get(self, name):
        return self.banks[name]

    def deps(self, name):
        return self.cons_tok.get(name, [])

    def consumed(self, name, token):
        self.cons_tok[name] = [token]

    def also_consumed(self, name, token):
        self.cons_tok.setdefault(name, []).append(token)


def make_ctx(nc, es):
    ctx = Ctx(nc, es)
    pp = PsumPool(ctx)
    for nm in ("w0", "w1", "w2", "w3", "b0", "s1", "s2"):
        pp.add(nm, [128, 512], F32)
    pp.add("tp", [128, 1024], BF16)
    ctx.psum_pool = pp
    return ctx




def emit_ln(ctx, consts, src, dst, gamma_sb=None, beta_sb=None, out_f32=None):
    pp = ctx.psum_pool
    sq = ctx.scr_b(7)
    mu_b = ctx.scr_f(4)
    rs_b = ctx.scr_f(5)
    rowbuf = ctx.sbuf("scr_row", [1, 3 * TT], F32)
    mu_row = rowbuf[:, 0:TT]
    msq_row = rowbuf[:, TT:2 * TT]
    mu2_row = rowbuf[:, 2 * TT:3 * TT]
    var_row = msq_row
    sd_row = msq_row
    rstd_row = mu2_row
    t_f = ctx.scr_f(2)

    ln_prev = []
    for tt in range(NTT):
        sl = slice(tt * TT, (tt + 1) * TT)
        s1 = pp.get("s1")
        s2 = pp.get("s2")
        tok_sq_prev = None
        mm1 = mm2 = None
        for c in range(NC):
            src_ap = src(c)[:, sl]
            tsq = ctx.emit("act", lambda a, o=sq[:], i=src_ap: a.activation(o, i, AF.Square),
                           deps=[tok_sq_prev] if tok_sq_prev else [], token=True)
            mm1 = ctx.emit("pe", lambda t, o=s1[0:1, 0:TT], l=consts.ones_col_bf[:], r=src_ap,
                           st=(c == 0), sp=(c == NC - 1):
                           t.matmul(o, l, r, start=st, stop=sp),
                           deps=(pp.deps("s1") if c == 0 else []), token=(c == NC - 1))
            mm2 = ctx.emit("pe", lambda t, o=s2[0:1, 0:TT], l=consts.ones_col_bf[:], r=sq[:],
                           st=(c == 0), sp=(c == NC - 1):
                           t.matmul(o, l, r, start=st, stop=sp),
                           deps=[tsq] + (pp.deps("s2") if c == 0 else []), token=True)
            tok_sq_prev = mm2
        tmu = ctx.emit("dve", lambda v: v.tensor_scalar(mu_row, s1[0:1, 0:TT], 1.0 / D, None, ALU.mult),
                       deps=[mm1] + ln_prev, token=True)
        pp.consumed("s1", tmu)
        tms = ctx.emit("dve", lambda v: v.tensor_scalar(msq_row, s2[0:1, 0:TT], 1.0 / D, None, ALU.mult),
                       deps=[mm2], token=True)
        pp.consumed("s2", tms)
        tm2 = ctx.emit("act", lambda a: a.activation(mu2_row, mu_row, AF.Square), deps=[tmu] + ln_prev, token=True)
        tvar = ctx.emit("dve", lambda v: v.tensor_sub(var_row, msq_row, mu2_row), deps=[tms, tm2], token=True)
        tve = ctx.emit("dve", lambda v: v.tensor_scalar(var_row, var_row, EPS_LN, None, ALU.add), deps=[tvar], token=True)
        tsd = ctx.emit("act", lambda a: a.activation(sd_row, var_row, AF.Sqrt), deps=[tve], token=True)
        trs = ctx.emit("dve", lambda v: v.reciprocal(rstd_row, sd_row), deps=[tsd], token=True)
        b0 = pp.get("b0")
        w3 = pp.get("w3")
        tb1 = ctx.emit("pe", lambda t: t.matmul(b0[:, 0:TT], consts.ones_row_f, mu_row, start=True, stop=True),
                       deps=[tmu] + pp.deps("b0") + consts.ready, token=True)
        tb1e = ctx.emit("dve", lambda v: v.tensor_copy(mu_b[:], b0[:, 0:TT]), deps=[tb1] + ln_prev, token=True)
        pp.consumed("b0", tb1e)
        tb2 = ctx.emit("pe", lambda t: t.matmul(w3[:, 0:TT], consts.ones_row_f, rstd_row, start=True, stop=True),
                       deps=[trs] + pp.deps("w3"), token=True)
        tb2e = ctx.emit("dve", lambda v: v.tensor_copy(rs_b[:], w3[:, 0:TT]), deps=[tb2], token=True)
        pp.consumed("w3", tb2e)
        prev_t3 = []
        for c in range(NC):
            src_ap = src(c)[:, sl]
            if out_f32 is not None:
                dst_ap = out_f32(c)[:, sl]
            else:
                dst_ap = dst(c)[:, sl]
            t1 = ctx.emit("dve", lambda v, i=src_ap: v.tensor_sub(t_f[:], i, mu_b[:]),
                          deps=[tb1e, tb2e] + prev_t3, token=True)
            if gamma_sb is not None:
                t2 = ctx.emit("dve", lambda v: v.tensor_mul(t_f[:], t_f[:], rs_b[:]), deps=[t1], token=True)
                t3 = ctx.emit("dve", lambda v, o=dst_ap, g=gamma_sb[:, c:c + 1], b=beta_sb[:, c:c + 1]:
                              v.tensor_scalar(o, t_f[:], g, b, ALU.mult, ALU.add),
                              deps=[t2], token=True)
            else:
                t3 = ctx.emit("dve", lambda v, o=dst_ap: v.tensor_mul(o, t_f[:], rs_b[:]),
                              deps=[t1], token=True)
            prev_t3 = [t3]
        ln_prev = [t3, tsd, tb2, tb1e, tb2e]
    return t3




def build_l0_attn():
    nc = bass.Bass("TRN2", target_bir_lowering=False, debug=False)
    aps = {}
    aps["X"] = nc.dram_tensor("X", [S, D], F32, kind="ExternalInput").ap()
    aps["W_in"] = nc.dram_tensor("W_in", [D, 3 * D], F32, kind="ExternalInput").ap()
    aps["b_in"] = nc.dram_tensor("b_in", [3 * D], F32, kind="ExternalInput").ap()
    aps["W_out"] = nc.dram_tensor("W_out", [D, D], F32, kind="ExternalInput").ap()
    aps["b_out"] = nc.dram_tensor("b_out", [D], F32, kind="ExternalInput").ap()
    aps["gamma"] = nc.dram_tensor("gamma", [D], F32, kind="ExternalInput").ap()
    aps["beta"] = nc.dram_tensor("beta", [D], F32, kind="ExternalInput").ap()
    aps["c_ident"] = nc.dram_tensor("c_ident", [128, 128], F32, kind="ExternalInput").ap()
    aps["c_swap"] = nc.dram_tensor("c_swap", [128, 128], F32, kind="ExternalInput").ap()
    aps["c_cos"] = nc.dram_tensor("c_cos", [1024, S], F32, kind="ExternalInput").ap()
    aps["c_sinp"] = nc.dram_tensor("c_sinp", [1024, S], F32, kind="ExternalInput").ap()
    xout = nc.dram_tensor("xout", [NC, 128, S], BF16, kind="ExternalOutput").ap()
    xpark = nc.dram_tensor("xpark", [NC, 128, S], BF16, kind="Internal").ap()
    dbg = None
    if os.environ.get("BK_DBG", "0") == "1":
        dbg = {
            "xn": nc.dram_tensor("d_xn", [NC, 128, S], BF16, kind="ExternalOutput").ap(),
            "A": nc.dram_tensor("d_A", [128, 16], F32, kind="ExternalOutput").ap(),
            "Y": nc.dram_tensor("d_Y", [NC, 128, S], BF16, kind="ExternalOutput").ap(),
            "rows": nc.dram_tensor("d_rows", [1, 2048], F32, kind="ExternalOutput").ap(),
            "wb": nc.dram_tensor("d_wb", [128, 512], F32, kind="ExternalOutput").ap(),
            "P": nc.dram_tensor("d_P", [128, 2048], BF16, kind="ExternalOutput").ap(),
            "pw": nc.dram_tensor("d_pw", [128, 512], F32, kind="ExternalOutput").ap(),
            "acol": nc.dram_tensor("d_acol", [128, 1], F32, kind="ExternalOutput").ap(),
        }

    es = ExitStack()
    with es:
        ctx = make_ctx(nc, es)
        consts = Consts(ctx, aps)
        emit_l0_attn(ctx, consts, aps, None, xout, xpark, first=True, dbg=dbg)
        ctx.barrier(["sp"])
        ctx.build()
    return nc


def emit_l0_attn(ctx, consts, aps, xin, xout, xpark, first, dbg=None):
    nc = ctx.nc
    pp = ctx.psum_pool
    swap_bf = ctx.sbuf("c_swap_bf", [128, 128], BF16)
    tsw = ctx.dma("pool", swap_bf[:], aps["c_swap"])

    xn = ctx.sbuf("big_a", [128, NC * S], BF16)
    xn_c = lambda c: xn[:, c * S:(c + 1) * S]
    x = ctx.sbuf("big_b", [128, NC * S], BF16)
    x_c = lambda c: x[:, c * S:(c + 1) * S]
    big_c = ctx.sbuf("big_c", [128, 20480], BF16)

    ent = big_c[:, 16384:16384 + D]
    entry_done = []
    x3 = x.rearrange("p (c s) -> p c s", c=NC)
    if first:
        for sc in range(16):
            tld = ctx.dma("pool", ent[:], aps["X"][sc * 128:(sc + 1) * 128, :],
                          deps=entry_done)
            entry_done = []
            tpb = pp.get("tp")
            for half in range(2):
                ttp = None
                for k in range(8):
                    dc = half * 8 + k
                    ttp = ctx.emit("pe", lambda t, i=ent[:, dc * 128:(dc + 1) * 128],
                                   o=tpb[:, k * 128:(k + 1) * 128]:
                                   t.transpose(o, i, consts.ident_bf[:]),
                                   deps=[tld] + consts.ready + (pp.deps("tp") if k == 0 else []),
                                   token=(k == 7))
                tev = ctx.emit("act", lambda a,
                               o=x3[:, half * 8:(half + 1) * 8, sc * 128:(sc + 1) * 128],
                               i=tpb[:].rearrange("p (k f) -> p k f", k=8):
                               a.copy(o, i),
                               deps=[ttp], token=True)
                entry_done.append(tev)
                pp.consumed("tp", tev)
    else:
        for c in range(NC):
            entry_done.append(ctx.dma("pool", x_c(c), xin[c]))

    park_tok = [ctx.dma_group("pool", [(xpark[c], x_c(c)) for c in range(NC)],
                              deps=entry_done)]

    gamma_sb, tg = load_pervec(ctx, consts, "v_gamma", aps["gamma"], D)
    beta_sb, tb = load_pervec(ctx, consts, "v_beta", aps["beta"], D)
    bin_sb, tbin = load_pervec(ctx, consts, "v_bin", aps["b_in"], 3 * D)
    bout_sb, tbo = load_pervec(ctx, consts, "v_bout", aps["b_out"], D)
    ctx.barrier(["pe", "dve", "act"])
    tln = emit_ln(ctx, consts, x_c, xn_c, gamma_sb, beta_sb)
    ctx.barrier(["pe", "dve", "act", "pool", "sp"])
    if dbg is not None:
        for c in range(NC):
            ctx.dma("pool", dbg["xn"][c], xn_c(c))

    A_acc = ctx.sbuf("A_acc", [128, 16], F32)
    tA = ctx.emit("dve", lambda v: v.memset(A_acc[:], 0.0), token=True)
    kw = big_c[:, 0:2 * NC * 512]
    kw2 = x[:, 16384:32768]
    kw_k = lambda d_: kw[:, d_ * 512:(d_ + 1) * 512]
    kw_v = lambda d_: kw[:, (NC + d_) * 512:(NC + d_ + 1) * 512]
    kw_kp = lambda p_, d_: (kw, kw2)[p_][:, d_ * 512:(d_ + 1) * 512]
    kw_vp = lambda p_, d_: (kw, kw2)[p_][:, (NC + d_) * 512:(NC + d_ + 1) * 512]
    ktile_q = (ctx.scr_b(0), ctx.scr_b(8))
    vtile_q = (ctx.scr_b(1), ctx.scr_b(9))
    krot_q = (ctx.scr_b(2), ctx.scr_b(10))
    vrot_q = (ctx.scr_b(3), ctx.scr_b(11))
    costl_q = (ctx.scr_b(4), ctx.scr_b(4))
    sintl_q = (ctx.scr_b(5), ctx.scr_b(5))
    tf1_q = (ctx.scr_f(0), ctx.scr_f(0))
    tf2_q = (ctx.scr_f(1), ctx.scr_f(1))
    sqt_q = (ctx.scr_b(6), ctx.scr_b(12))
    pw = ctx.scr_f(2)
    acol = ctx.sbuf("kv_acol", [128, 1], F32)
    wrow3 = ctx.sbuf("scr_row", [1, 3 * 512], F32)
    w_b = ctx.scr_f(3)
    p_c = lambda j, ts: x[:, j * 1024 + ts * 512: j * 1024 + (ts + 1) * 512]
    ctx.fence("dve", park_tok)
    sq_banks = {(0, 0): "s1", (0, 1): "s2", (1, 0): "b0", (1, 1): "w3"}

    last_kw_use = {0: [], 1: []}
    last_p_use = [tA]
    cos_use = {0: [], 1: []}
    fin_prev = []
    fin_last = []
    rot_prev = {0: [], 1: []}
    kv_lds = {}

    def kv_load(gi):
        jg_ = gi % 4
        p_ = gi % 2
        lk = ctx.dma_group("pool", [
            (kw_kp(p_, d_), aps["W_in"][d_ * 128:(d_ + 1) * 128, D + jg_ * 512: D + (jg_ + 1) * 512])
            for d_ in range(NC)], deps=last_kw_use[p_] + park_tok)
        lv = ctx.dma_group("pool", [
            (kw_vp(p_, d_), aps["W_in"][d_ * 128:(d_ + 1) * 128, 2 * D + jg_ * 512: 2 * D + (jg_ + 1) * 512])
            for d_ in range(NC)])
        last_kw_use[p_] = []
        kv_lds[gi] = (lk, lv)

    kv_load(0)
    kv_load(1)
    for tpass in range(2 if os.environ.get('BK_TPASS', '2') == '2' else 1):
        sq_done = {}
        for jg in range(4):
            gi = tpass * 4 + jg
            p_ = gi % 2
            if gi + 1 <= 7 and gi + 1 not in kv_lds:
                kv_load(gi + 1)
            ld_k, ld_v = kv_lds[gi]
            for jj in range(4):
                pair = jg * 4 + jj
                for tsub in range(2):
                    q_ = (pair * 2 + tsub) % 2
                    ktile = ktile_q[q_]
                    vtile = vtile_q[q_]
                    krot = krot_q[q_]
                    vrot = vrot_q[q_]
                    costl = costl_q[q_]
                    sintl = sintl_q[q_]
                    sqt = sqt_q[q_]
                    tf1 = tf1_q[q_]
                    tf2 = tf2_q[q_]
                    tt = tpass * 2 + tsub
                    sl = slice(tt * 512, (tt + 1) * 512)
                    pk = pp.get("w0")
                    pv = pp.get("w1")
                    mmk = mmv = None
                    for d_ in range(NC):
                        mmk = ctx.emit("pe", lambda t, o=pk[:], l=kw_kp(p_, d_)[:, jj * 128:(jj + 1) * 128],
                                       r=xn_c(d_)[:, sl], st=(d_ == 0), sp=(d_ == NC - 1):
                                       t.matmul(o, l, r, start=st, stop=sp),
                                       deps=([ld_k, tln] + (pp.deps("w0") if d_ == 0 else [])),
                                       token=(d_ == NC - 1))
                    for d_ in range(NC):
                        mmv = ctx.emit("pe", lambda t, o=pv[:], l=kw_vp(p_, d_)[:, jj * 128:(jj + 1) * 128],
                                       r=xn_c(d_)[:, sl], st=(d_ == 0), sp=(d_ == NC - 1):
                                       t.matmul(o, l, r, start=st, stop=sp),
                                       deps=([ld_v] + (pp.deps("w1") if d_ == 0 else [])),
                                       token=(d_ == NC - 1))
                    last_kw_use[p_].extend([mmk, mmv])
                    tke = ctx.emit("act", lambda a, o=ktile[:], p=pk[:], b=bin_sb[:, 16 + pair:17 + pair]:
                                   a.activation(o, p, AF.Identity, bias=b),
                                   deps=[mmk, tbin] + rot_prev[q_] + last_p_use, token=True)
                    pp.consumed("w0", tke)
                    tve = ctx.emit("act", lambda a, o=vtile[:], p=pv[:], b=bin_sb[:, 32 + pair:33 + pair]:
                                   a.activation(o, p, AF.Identity, bias=b),
                                   deps=[mmv, tbin], token=True)
                    pp.consumed("w1", tve)
                    if pair < 8:
                        tlc = ctx.dma_group("pool", [
                            (costl[:], aps["c_cos"][pair * 128:(pair + 1) * 128, sl]),
                            (sintl[:], aps["c_sinp"][pair * 128:(pair + 1) * 128, sl])],
                            deps=cos_use[0] + cos_use[1])
                        tls = tlc
                        cos_use[0] = []
                        cos_use[1] = []
                        psw = pp.get("w2")
                        tkv_ready = []
                        for (tin, tout) in ((ktile, krot), (vtile, vrot)):
                            tswm = ctx.emit("pe", lambda t, i=tin[:]:
                                            t.matmul(psw[:], swap_bf[:], i, start=True, stop=True),
                                            deps=[tke, tve, tsw] + pp.deps("w2"), token=True)
                            t1 = ctx.emit("dve", lambda v, o=tf1[:], s=sintl[:]: v.tensor_mul(o, psw[:], s),
                                          deps=[tswm, tls] + rot_prev[0] + rot_prev[1], token=True)
                            pp.consumed("w2", t1)
                            t2 = ctx.emit("dve", lambda v, i=tin[:], o=tf2[:], s=costl[:]: v.tensor_mul(o, i, s),
                                          deps=[tlc] + rot_prev[0] + rot_prev[1], token=True)
                            t3 = ctx.emit("dve", lambda v, o=tout[:], a_=tf1[:], b_=tf2[:]: v.tensor_add(o, a_, b_),
                                          deps=[t1, t2], token=True)
                            tkv_ready.append(t3)
                            cos_use[0].append(t2)
                            rot_prev[q_] = [t3]
                        ksrc, vsrc = krot, vrot
                    else:
                        tkv_ready = [tke, tve]
                        ksrc, vsrc = ktile, vtile
                    first_pair = (pair == 0)
                    last_pair = (pair == 15)
                    for kv_i, tsrc in ((0, ktile), (1, vtile)):
                        bname = sq_banks[(kv_i, tsub)]
                        sb_ = pp.get(bname)
                        tsq = ctx.emit("act", lambda a, i=tsrc[:], o=sqt[:]: a.activation(o, i, AF.Square),
                                       deps=[tke, tve] + ([sq_done.get(("mm", q_))] if sq_done.get(("mm", q_)) else []),
                                       token=True)
                        msq = ctx.emit("pe", lambda t, o=sb_[0:1, 0:512], i=sqt[:], st=first_pair, sp=last_pair:
                                       t.matmul(o, consts.ones_col_bf[:], i, start=st, stop=sp),
                                       deps=[tsq] + (pp.deps(bname) if first_pair else []),
                                       token=True)
                        sq_done[("mm", q_)] = msq
                        if last_pair:
                            sq_done[(kv_i, tsub)] = msq
                    tp_ = ctx.emit("dve", lambda v, a=ksrc[:], b=vsrc[:], o=p_c(pair, tsub):
                                   v.tensor_mul(o, a, b), deps=tkv_ready + last_p_use, token=True)
                    last_p_use = []
                    rot_prev[q_] = rot_prev[q_] + [tp_]
                    last_kw_use[p_].append(tp_)
        for tsub in range(2):
            skb = pp.get(sq_banks[(0, tsub)])
            svb = pp.get(sq_banks[(1, tsub)])
            r1 = wrow3[:, 0:512]
            r2 = wrow3[:, 512:1024]
            r3 = wrow3[:, 1024:1536]
            ts1 = ctx.emit("act", lambda a, i=skb[0:1, 0:512]: a.activation(r1, i, AF.Sqrt),
                           deps=[sq_done[(0, tsub)]] + fin_prev, token=True)
            ts2 = ctx.emit("act", lambda a, i=svb[0:1, 0:512]: a.activation(r2, i, AF.Sqrt),
                           deps=[sq_done[(1, tsub)]] + fin_prev, token=True)
            tr1 = ctx.emit("dve", lambda v: v.reciprocal(r1, r1), deps=[ts1], token=True)
            pp.consumed(sq_banks[(0, tsub)], tr1)
            tr2 = ctx.emit("dve", lambda v: v.reciprocal(r2, r2), deps=[ts2], token=True)
            pp.consumed(sq_banks[(1, tsub)], tr2)
            tw = ctx.emit("dve", lambda v: v.tensor_mul(r3, r1, r2), deps=[tr1, tr2], token=True)
            fin_prev = [tw]
            bb = pp.get("w2")
            tbw = ctx.emit("pe", lambda t: t.matmul(bb[:], consts.ones_row_f, r3, start=True, stop=True),
                           deps=[tw] + pp.deps("w2"), token=True)
            tbe = ctx.emit("dve", lambda v: v.tensor_copy(w_b[:], bb[:]), deps=[tbw] + fin_last, token=True)
            pp.consumed("w2", tbe)
            pair_range = range(15, 16) if os.environ.get("BK_NPAIR") == "1" else range(16)
            for pair in pair_range:
                t1 = ctx.emit("dve", lambda v, p=p_c(pair, tsub): v.tensor_mul(pw[:], p, w_b[:]),
                              deps=[tbe] + fin_last, token=True)
                t2 = ctx.emit("dve", lambda v: v.reduce_sum(acol[:], pw[:], axis=AX), deps=[t1], token=True)
                t3 = ctx.emit("dve", lambda v, o=A_acc[:, pair:pair + 1]:
                              v.tensor_add(o, o, acol[:]), deps=[t2], token=True)
                last_p_use.append(t1)
                fin_last = [t3]
    if dbg is not None:
        ctx.barrier(["pool", "dve"])
        hosts = [tf1, tf2, pw, krot]
        for i, bn in enumerate(["s1", "s2", "b0", "w3"]):
            ctx.emit("dve", lambda v, o=hosts[i][0:1, 0:512], p=pp.get(bn)[0:1, 0:512]:
                     v.tensor_copy(o, p), token=True)
        ctx.barrier(["pool"])
        for i in range(4):
            ctx.dma("pool", dbg["rows"][:, i * 512:(i + 1) * 512], hosts[i][0:1, 0:512])
        ctx.dma("pool", dbg["wb"], w_b[:])
        ctx.dma("pool", dbg["pw"], pw[:])
        ctx.dma("pool", dbg["acol"], acol[:])
        ctx.dma("pool", dbg["P"], x[:, 0:2048])
        ctx.dma("pool", dbg["A"], A_acc[:])
        ctx.barrier(["dve"])
    tA = ctx.emit("dve", lambda v: v.tensor_scalar(A_acc[:], A_acc[:], 0.5, 0.5, ALU.mult, ALU.add),
                  deps=fin_last, token=True)

    y = x
    y_c = x_c
    last_kw_all = last_kw_use[0] + last_kw_use[1]
    for fw in range(4):
        ld_q = ctx.dma_group("pool", [
            (kw_k(d_), aps["W_in"][d_ * 128:(d_ + 1) * 128, fw * 512:(fw + 1) * 512])
            for d_ in range(NC)], deps=last_kw_all + park_tok)
        last_kw_all = []
        for ff in range(4):
            fq = fw * 4 + ff
            for tt in range(NTT):
                sl = slice(tt * TT, (tt + 1) * TT)
                bank = ("w0", "w1")[(ff * NTT + tt) % 2]
                pq = pp.get(bank)
                mmq = None
                for d_ in range(NC):
                    mmq = ctx.emit("pe", lambda t, o=pq[:], l=kw_k(d_)[:, ff * 128:(ff + 1) * 128],
                                   r=xn_c(d_)[:, sl], st=(d_ == 0), sp=(d_ == NC - 1):
                                   t.matmul(o, l, r, start=st, stop=sp),
                                   deps=([ld_q, tA] + (pp.deps(bank) if d_ == 0 else [])),
                                   token=(d_ == NC - 1))
                last_kw_all.append(mmq)
                tev = ctx.emit("dve", lambda v, o=y_c(fq)[:, sl], b=bin_sb[:, fq:fq + 1],
                               a=A_acc[:, fq:fq + 1], p=pq[:]:
                               v.tensor_scalar(o, p, b, a, ALU.add, ALU.mult),
                               deps=[mmq, tA], token=True)
                pp.consumed(bank, tev)

    if dbg is not None:
        for c in range(NC):
            ctx.dma("pool", dbg["Y"][c], y_c(c))
    ctx.barrier(["pe"])
    xnew = xn
    xnew_c = xn_c
    xo0 = ctx.scr_b(4)
    xo1 = ctx.scr_b(5)
    t1f = ctx.scr_f(3)
    xo_use = {0: [], 1: []}
    prev_tb = []
    for fw in range(4):
        ld_o = ctx.dma_group("pool", [
            (kw_v(d_), aps["W_out"][d_ * 128:(d_ + 1) * 128, fw * 512:(fw + 1) * 512])
            for d_ in range(NC)], deps=last_kw_all)
        last_kw_all = []
        for ff in range(4):
            fo = fw * 4 + ff
            for tt in range(NTT):
                sl = slice(tt * TT, (tt + 1) * TT)
                par = (ff * NTT + tt) % 2
                xo = (xo0, xo1)[par]
                tldx = ctx.dma("pool", xo[:], xpark[fo, :, sl], deps=xo_use[par])
                xo_use[par] = []
                bank = ("w0", "w1")[par]
                po = pp.get(bank)
                mmo = None
                for d_ in range(NC):
                    mmo = ctx.emit("pe", lambda t, o=po[:], l=kw_v(d_)[:, ff * 128:(ff + 1) * 128],
                                   r=y_c(d_)[:, sl], st=(d_ == 0), sp=(d_ == NC - 1):
                                   t.matmul(o, l, r, start=st, stop=sp),
                                   deps=([ld_o] + (pp.deps(bank) if d_ == 0 else [])),
                                   token=(d_ == NC - 1))
                last_kw_all.append(mmo)
                ta = ctx.emit("act", lambda a, p=po[:], b=bout_sb[:, fo:fo + 1]:
                              a.activation(t1f[:], p, AF.Identity, bias=b),
                              deps=[mmo, tbo] + prev_tb, token=True)
                pp.consumed(bank, ta)
                tb_ = ctx.emit("dve", lambda v, o=xnew_c(fo)[:, sl], xold=xo[:]:
                               v.scalar_tensor_tensor(o, xold, 2.0, t1f[:], ALU.mult, ALU.add),
                               deps=[ta, tldx], token=True)
                xo_use[par].append(tb_)
                prev_tb = [tb_]

    ctx.barrier(["pe", "dve", "act"])
    emit_ln(ctx, consts, xnew_c, xnew_c)
    if xout is not None:
        ctx.barrier(["pool"])
        ctx.dma_group("pool", [(xout[c], xnew_c(c)) for c in range(NC)])





def np_consts():
    ident = np.eye(128, dtype=np.float32)
    swap = np.zeros((128, 128), dtype=np.float32)
    m = np.arange(128)
    swap[m ^ 1, m] = 1.0
    return ident, swap


def np_rotary_tables():
    rot = D // 2
    inv = 1.0 / (10000.0 ** (np.arange(0, rot, 2, dtype=np.float64) / rot))
    f = np.arange(S, dtype=np.float64)[None, :] * inv[:, None]
    cos = np.repeat(np.cos(f), 2, axis=0).astype(np.float32)
    sin = np.repeat(np.sin(f), 2, axis=0)
    sign = np.where(np.arange(rot) % 2 == 0, -1.0, 1.0)[:, None]
    sinp = (sin * sign).astype(np.float32)
    return np.ascontiguousarray(cos), np.ascontiguousarray(sinp)


def run_l0_attn(inputs, n_cores=8):
    nc = build_l0_attn()
    ident, swap = np_consts()
    cos, sinp = np_rotary_tables()
    X = np.ascontiguousarray(inputs["L0_attn_X"])
    in_maps = []
    for b in range(n_cores):
        in_maps.append({
            "X": X[b],
            "W_in": inputs["L0_attn_W_in"],
            "b_in": inputs["L0_attn_b_in"],
            "W_out": inputs["L0_attn_W_out"],
            "b_out": inputs["L0_attn_b_out"],
            "gamma": inputs["L0_attn_gamma"],
            "beta": inputs["L0_attn_beta"],
            "c_ident": ident, "c_swap": swap, "c_cos": cos, "c_sinp": sinp,
        })
    res = run_bass_kernel_spmd(nc, in_maps, core_ids=list(range(n_cores)))
    return [r["xout"] for r in res.results]


def build_fused():
    nc = bass.Bass("TRN2", target_bir_lowering=False, debug=False)
    aps = {}
    def inp(name, shape):
        aps[name] = nc.dram_tensor(name, shape, F32, kind="ExternalInput").ap()
    inp("X", [S, D]); inp("W_in", [D, 3 * D]); inp("b_in", [3 * D])
    inp("W_out", [D, D]); inp("b_out", [D]); inp("gamma", [D]); inp("beta", [D])
    inp("W_up", [INTER, D]); inp("W_gate", [INTER, D]); inp("W_down", [D, INTER])
    inp("SU_ug", [D]); inp("SU_dp", [INTER]); inp("Wscale_ug", [2 * INTER]); inp("Wscale_dp", [D])
    inp("memories", [M, H, HD, HD]); inp("memory_norms", [M, H, HD])
    inp("w_q", [D, D]); inp("w_k", [D, D]); inp("w_v", [D, D]); inp("w_o", [D, D]); inp("gate", [H])
    inp("wg", [NEXP, D]); inp("w1", [NEXP, D, 2 * FFN]); inp("w2", [NEXP, FFN, D])
    inp("c_ident", [128, 128]); inp("c_swap", [128, 128])
    inp("c_cos", [1024, S]); inp("c_sinp", [1024, S])
    inp("c_h128", [128, 128]); inp("c_maskT", [128, 128])
    out = nc.dram_tensor("out", [S, D], F32, kind="ExternalOutput").ap()
    xpark = nc.dram_tensor("xpark", [NC, 128, S], BF16, kind="Internal").ap()
    wd2_hbm = nc.dram_tensor("wd2_hbm", [16, 128, INTER], BF16, kind="Internal").ap()
    za_hbm = nc.dram_tensor("za_hbm", [NTT, 128, NCI * TT], BF16, kind="Internal").ap()
    acc_hbm = nc.dram_tensor("acc_hbm", [NC, 128, S], F32, kind="Internal").ap()
    crow_hbm = nc.dram_tensor("crow_hbm", [NEXP, S], F32, kind="Internal").ap()
    es = ExitStack()
    with es:
        ctx = make_ctx(nc, es)
        consts = Consts(ctx, aps)
        emit_l0_attn(ctx, consts, aps, None, None, xpark, first=True)
        ctx.barrier()
        emit_l0_ffn(ctx, consts, aps, wd2_hbm, za_hbm)
        ctx.barrier()
        emit_l1_attn(ctx, consts, aps)
        ctx.barrier()
        emit_l1_moe(ctx, consts, aps, acc_hbm, crow_hbm)
        emit_final_out(ctx, consts, out, acc_hbm)
        ctx.barrier(["sp"])
        ctx.build()
    return nc


def kernel_fused(**inputs):
    n = 8
    nc = build_fused()
    ident, swap = np_consts()
    cos, sinp = np_rotary_tables()
    X = np.ascontiguousarray(np.asarray(inputs["L0_attn_X"]))
    in_maps = []
    for b in range(n):
        in_maps.append({
            "X": X[b],
            "W_in": inputs["L0_attn_W_in"], "b_in": inputs["L0_attn_b_in"],
            "W_out": inputs["L0_attn_W_out"], "b_out": inputs["L0_attn_b_out"],
            "gamma": inputs["L0_attn_gamma"], "beta": inputs["L0_attn_beta"],
            "W_up": inputs["L0_ffn_W_up"], "W_gate": inputs["L0_ffn_W_gate"],
            "W_down": inputs["L0_ffn_W_down"],
            "SU_ug": inputs["L0_ffn_SU_ug"], "SU_dp": inputs["L0_ffn_SU_dp"],
            "Wscale_ug": inputs["L0_ffn_Wscale_ug"], "Wscale_dp": inputs["L0_ffn_Wscale_dp"],
            "memories": inputs["L1_attn_memories"], "memory_norms": inputs["L1_attn_memory_norms"],
            "w_q": inputs["L1_attn_w_q"], "w_k": inputs["L1_attn_w_k"],
            "w_v": inputs["L1_attn_w_v"], "w_o": inputs["L1_attn_w_o"],
            "gate": inputs["L1_attn_gate"],
            "wg": inputs["L1_ffn_wg"], "w1": inputs["L1_ffn_w1"], "w2": inputs["L1_ffn_w2"],
            "c_ident": ident, "c_swap": swap, "c_cos": cos, "c_sinp": sinp,
            "c_h128": np_h128(), "c_maskT": np_maskT(),
        })
    res = run_bass_kernel_spmd(nc, in_maps, core_ids=list(range(n)))
    return np.stack([np.asarray(r["out"], dtype=np.float32) for r in res.results], axis=0)


def kernel(**inputs):
    if os.environ.get("BK_UNFUSED", "0") != "1":
        return kernel_fused(**inputs)
    n = 8
    X = np.ascontiguousarray(np.asarray(inputs["L0_attn_X"]))
    x1 = run_l0_attn(inputs, n_cores=n)
    x2 = run_l0_ffn(x1, inputs, n_cores=n)
    x3 = run_l1_attn(x2, inputs, n_cores=n)
    outs = run_l1_moe(x3, inputs, n_cores=n)
    return np.stack([np.asarray(o, dtype=np.float32) for o in outs], axis=0)



INTER = 8192
NCI = INTER // 128


def build_l0_ffn():
    nc = bass.Bass("TRN2", target_bir_lowering=False, debug=False)
    aps = {}
    aps["W_up"] = nc.dram_tensor("W_up", [INTER, D], F32, kind="ExternalInput").ap()
    aps["W_gate"] = nc.dram_tensor("W_gate", [INTER, D], F32, kind="ExternalInput").ap()
    aps["W_down"] = nc.dram_tensor("W_down", [D, INTER], F32, kind="ExternalInput").ap()
    aps["SU_ug"] = nc.dram_tensor("SU_ug", [D], F32, kind="ExternalInput").ap()
    aps["SU_dp"] = nc.dram_tensor("SU_dp", [INTER], F32, kind="ExternalInput").ap()
    aps["Wscale_ug"] = nc.dram_tensor("Wscale_ug", [2 * INTER], F32, kind="ExternalInput").ap()
    aps["Wscale_dp"] = nc.dram_tensor("Wscale_dp", [D], F32, kind="ExternalInput").ap()
    aps["c_ident"] = nc.dram_tensor("c_ident", [128, 128], F32, kind="ExternalInput").ap()
    aps["c_h128"] = nc.dram_tensor("c_h128", [128, 128], F32, kind="ExternalInput").ap()
    xin = nc.dram_tensor("xin", [NC, 128, S], BF16, kind="ExternalInput").ap()
    xout = nc.dram_tensor("xout", [NC, 128, S], BF16, kind="ExternalOutput").ap()
    wd2_hbm = nc.dram_tensor("wd2_hbm", [16, 128, INTER], BF16, kind="Internal").ap()
    za_hbm = nc.dram_tensor("za_hbm", [NTT, 128, NCI * TT], BF16, kind="Internal").ap()

    es = ExitStack()
    with es:
        ctx = make_ctx(nc, es)
        consts = Consts(ctx, aps)
        x = ctx.sbuf("big_a", [128, NC * S], BF16)
        x_c = lambda c: x[:, c * S:(c + 1) * S]
        ld = ctx.dma_group("pool", [(x_c(c), xin[c]) for c in range(NC)])
        ctx.barrier()
        emit_l0_ffn(ctx, consts, aps, wd2_hbm, za_hbm)
        ctx.barrier(["pool"])
        ctx.dma_group("pool", [(xout[c], x_c(c)) for c in range(NC)])
        ctx.barrier(["sp"])
        ctx.build()
    return nc


def emit_l0_ffn(ctx, consts, aps, wd2_hbm, za_hbm):
    nc = ctx.nc
    pp = ctx.psum_pool
    x = ctx.sbuf("big_a", [128, NC * S], BF16)
    x_c = lambda c: x[:, c * S:(c + 1) * S]
    bb = ctx.sbuf("big_b", [128, NC * S], BF16)
    bc = ctx.sbuf("big_c", [128, 20480], BF16)
    pw_a = bb[:, 0:16384].bitcast(F32)
    pw_b = bc[:, 0:16384].bitcast(F32)
    stW1 = bb[:, 16384:24576]
    stW2 = bb[:, 24576:32768]
    TTF = 256
    p1_a = bc[:, 0:8192].bitcast(F32)
    p1_b = bc[:, 8192:16384].bitcast(F32)
    x1_c = lambda c: bb[:, c * S:(c + 1) * S]
    za_c = lambda c: bb[:, c * TT:(c + 1) * TT]
    h128 = ctx.sbuf("c_h128_bf", [128, 128], BF16)
    th = ctx.dma("pool", h128[:], aps["c_h128"])

    su_ug, t1v = load_pervec(ctx, consts, "v_suug", aps["SU_ug"], D)
    wsug, t2v = load_pervec(ctx, consts, "v_wsug", aps["Wscale_ug"], 2 * INTER)
    sudp, t3v = load_pervec(ctx, consts, "v_sudp", aps["SU_dp"], INTER)
    wsdp, t4v = load_pervec(ctx, consts, "v_wsdp", aps["Wscale_dp"], D)
    su2 = ctx.sbuf("v_su2", [128, NCI], F32)
    sg = ctx.sbuf("v_sg", [128, NCI], F32)
    sdp = ctx.sbuf("v_sdp", [128, NC], F32)
    c2048 = 1.0 / np.sqrt(2048.0)
    c8192 = 1.0 / np.sqrt(8192.0)
    tv1 = ctx.emit("dve", lambda v: v.tensor_scalar(su2[:], wsug[:, 0:NCI], c2048, None, ALU.mult),
                   deps=[t2v], token=True)
    tv2 = ctx.emit("dve", lambda v: v.tensor_mul(su2[:], su2[:], sudp[:]), deps=[tv1, t3v], token=True)
    tv3 = ctx.emit("dve", lambda v: v.tensor_scalar(sg[:], wsug[:, NCI:2 * NCI], c2048, None, ALU.mult),
                   deps=[t2v], token=True)
    tv4 = ctx.emit("dve", lambda v: v.tensor_scalar(sdp[:], wsdp[:], c8192, None, ALU.mult),
                   deps=[t4v], token=True)

    def butterfly(buf_cur, buf_nxt, h, fcols, deps, tag):
        cv = buf_cur.rearrange("p (a two b f) -> p a two b f", two=2, b=h, f=fcols)
        nv = buf_nxt.rearrange("p (a two b f) -> p a two b f", two=2, b=h, f=fcols)
        hf = fcols // 2
        toks = []
        for sign in range(2):
            for half in range(2):
                cs = slice(half * hf, (half + 1) * hf)
                eng = ("dve", "pool")[(sign + half) % 2]
                if sign == 0:
                    fn = (lambda E, s=cs: E.tensor_add(nv[:, :, 0, :, s], cv[:, :, 0, :, s], cv[:, :, 1, :, s]))
                else:
                    fn = (lambda E, s=cs: E.tensor_sub(nv[:, :, 1, :, s], cv[:, :, 0, :, s], cv[:, :, 1, :, s]))
                toks.append(ctx.emit(eng, fn, deps=deps, token=True))
        return toks

    last_pwa_use = []
    last_st_use = []
    for rc in range(16):
        ldw = ctx.dma("pool", pw_a, aps["W_down"][rc * 128:(rc + 1) * 128, :],
                      deps=last_pwa_use)
        toks = [ldw]
        cur, nxt = pw_a, pw_b
        for st in range(6):
            toks = butterfly(cur, nxt, 1 << st, 128, toks, f"wd{rc}_{st}")
            cur, nxt = nxt, cur
        tc1 = ctx.emit("dve", lambda v: v.tensor_copy(stW1[:, 0:4096], pw_a[:, 0:4096]),
                       deps=toks + last_st_use, token=True)
        tc2 = ctx.emit("dve", lambda v: v.tensor_copy(stW1[:, 4096:8192], pw_a[:, 4096:8192]),
                       deps=toks, token=True)
        last_pwa_use = [tc1, tc2]
        tpb = pp.get("tp")
        ev_toks = []
        for g8 in range(8):
            ttp = None
            for k in range(8):
                c = g8 * 8 + k
                ttp = ctx.emit("pe", lambda t, i=stW1[:, c * 128:(c + 1) * 128],
                               o=tpb[:, k * 128:(k + 1) * 128]:
                               t.transpose(o, i, consts.ident_bf[:]),
                               deps=[tc1, tc2] + consts.ready + (pp.deps("tp") if k == 0 else []),
                               token=(k == 7))
            tev = ctx.emit("act", lambda a,
                           o=stW2[:, g8 * 1024:(g8 + 1) * 1024],
                           i=tpb[:, 0:1024]: a.copy(o, i),
                           deps=[ttp] + last_st_use, token=True)
            pp.consumed("tp", tev)
            ev_toks.append(tev)
        tout = ctx.dma("pool", wd2_hbm[rc], stW2, deps=ev_toks)
        last_st_use = [tout]

    ctx.barrier(["pe", "dve", "act", "pool"])
    xsu = ctx.sbuf("scr_b256", [128, TTF], BF16)
    x1_done = []
    x1_prev = []
    for tt in range(S // TTF):
        sl = slice(tt * TTF, (tt + 1) * TTF)
        ev_prev = []
        prev_xsu = []
        for c in range(NC):
            bank = ("w0", "w1")[c % 2]
            ps = pp.get(bank)
            tx = ctx.emit("dve", lambda v, i=x_c(c)[:, sl], s=su_ug[:, c:c + 1]:
                          v.tensor_scalar(xsu[:], i, s, None, ALU.mult),
                          deps=[t1v] + prev_xsu, token=True)
            mmh = ctx.emit("pe", lambda t, o=ps[:, 0:TTF]: t.matmul(o, h128[:], xsu[:], start=True, stop=True),
                           deps=[tx, th] + pp.deps(bank), token=True)
            prev_xsu = [mmh]
            tev = ctx.emit("dve", lambda v, o=p1_a[:, c * TTF:(c + 1) * TTF], i=ps[:, 0:TTF]:
                           v.tensor_copy(o, i), deps=[mmh] + x1_prev, token=True)
            pp.consumed(bank, tev)
            ev_prev.append(tev)
        toks = ev_prev
        cur, nxt = p1_a, p1_b
        for st in range(4):
            toks = butterfly(cur, nxt, 1 << st, TTF, toks, f"x1_{tt}_{st}")
            cur, nxt = nxt, cur
        for c in range(NC):
            tcx = ctx.emit("dve", lambda v, o=x1_c(c)[:, sl], i=p1_a[:, c * TTF:(c + 1) * TTF]:
                           v.tensor_copy(o, i), deps=toks, token=True)
            x1_done.append(tcx)
        x1_prev = [x1_done[-1]]

    ctx.barrier(["pe", "dve", "act", "pool", "sp"])
    wnat_u = bc[:, 0:2048]
    wnat_g = bc[:, 2048:4096]
    wT = lambda par, w: bc[:, 4096 + par * 4096 + w * 2048: 4096 + par * 4096 + (w + 1) * 2048]
    tu_f0, tu_f1 = ctx.scr_f(0), ctx.scr_f(2)
    tg_f0, tg_f1 = ctx.scr_f(1), ctx.scr_f(3)
    zin0, zin1 = ctx.scr_b(0), ctx.scr_b(2)
    zev0, zev1 = ctx.scr_b(1), ctx.scr_b(3)
    prev_tr = {0: [], 1: []}
    prev_nat = []
    prev_zu = {0: [], 1: []}
    prev_zz = {0: [], 1: []}
    prev_ze = {0: [], 1: []}
    pend_z = []

    def flush_z(item):
        q, t_z, zin_, zev_, tt_, fi_ = item
        bz = ("b0", "s1")[q]
        psz = pp.get(bz)
        mmz = ctx.emit("pe", lambda t, i=zin_[:], o=psz[:]: t.matmul(o, h128[:], i, start=True, stop=True),
                       deps=[t_z, th] + pp.deps(bz), token=True)
        t_ze = ctx.emit("act", lambda a, o=zev_[:], p=psz[:]: a.copy(o, p), deps=[mmz] + prev_ze[q], token=True)
        pp.consumed(bz, t_ze)
        tzo = ctx.dma("pool", za_hbm[tt_][:, fi_ * TT:(fi_ + 1) * TT], zev_[:], deps=[t_ze])
        prev_zz[q] = [mmz]
        prev_ze[q] = [tzo]
    for fi in range(NCI):
        par = fi % 2
        ldn = ctx.dma_group("pool", [
            (wnat_u, aps["W_up"][fi * 128:(fi + 1) * 128, :]),
            (wnat_g, aps["W_gate"][fi * 128:(fi + 1) * 128, :])], deps=prev_nat)
        trt = []
        for w in range(2):
            wn = (wnat_u, wnat_g)[w]
            trt.append(ctx.dma_group(("sp", "act")[w], [
                (wT(par, w)[:, d_ * 128:(d_ + 1) * 128], wn[:, d_ * 128:(d_ + 1) * 128])
                for d_ in range(16)], deps=[ldn] + prev_tr[par], transpose=True))
        prev_nat = trt
        prev_tr[par] = []
        for tt in range(NTT):
            sl = slice(tt * TT, (tt + 1) * TT)
            q = (fi * NTT + tt) % 2
            bu = ("w0", "w2")[q]
            bg = ("w1", "w3")[q]
            psu = pp.get(bu)
            psg = pp.get(bg)
            tu_f = (tu_f0, tu_f1)[q]
            tg_f = (tg_f0, tg_f1)[q]
            zin = (zin0, zin1)[q]
            zev = (zev0, zev1)[q]
            mmu = mmg = None
            for d_ in range(16):
                mmu = ctx.emit("pe", lambda t, o=psu[:], l=wT(par, 0)[:, d_ * 128:(d_ + 1) * 128],
                               r=x1_c(d_)[:, sl], st=(d_ == 0), sp=(d_ == 15):
                               t.matmul(o, l, r, start=st, stop=sp),
                               deps=[trt[0]] + (pp.deps(bu) if d_ == 0 else []),
                               token=(d_ == 15))
            for d_ in range(16):
                mmg = ctx.emit("pe", lambda t, o=psg[:], l=wT(par, 1)[:, d_ * 128:(d_ + 1) * 128],
                               r=x1_c(d_)[:, sl], st=(d_ == 0), sp=(d_ == 15):
                               t.matmul(o, l, r, start=st, stop=sp),
                               deps=[trt[1]] + (pp.deps(bg) if d_ == 0 else []),
                               token=(d_ == 15))
            prev_tr[par].extend([mmu, mmg])
            t_u = ctx.emit("dve", lambda v, s=su2[:, fi:fi + 1], o=tu_f[:], p=psu[:]:
                           v.tensor_scalar(o, p, s, None, ALU.mult),
                           deps=[mmu, tv2] + prev_zu[q], token=True)
            pp.consumed(bu, t_u)
            t_g = ctx.emit("act", lambda a, s=sg[:, fi:fi + 1], o=tg_f[:], p=psg[:]:
                           a.activation(o, p, AF.Silu, scale=s),
                           deps=[mmg, tv3] + prev_zu[q], token=True)
            pp.consumed(bg, t_g)
            t_z = ctx.emit("dve", lambda v, o=zin[:], a_=tu_f[:], b_=tg_f[:]:
                           v.tensor_mul(o, a_, b_),
                           deps=[t_u, t_g] + prev_zz[q], token=True)
            pend_z.append((q, t_z, zin, zev, tt, fi))
            if len(pend_z) > 1:
                flush_z(pend_z.pop(0))
            prev_zu[q] = [t_z]

    for item in pend_z:
        flush_z(item)
    pend_z = []

    ctx.barrier(["pe", "dve", "act", "pool", "sp"])
    wd2b = lambda par: bc[:, par * 8192:(par + 1) * 8192]
    prev_wd = {0: [], 1: []}
    prev_za = []
    for tt in range(NTT):
        sl = slice(tt * TT, (tt + 1) * TT)
        lza = ctx.dma("pool", bb[:, 0:NCI * TT], za_hbm[tt], deps=prev_za)
        for f2 in range(NC):
            par = f2 % 2
            ldw = ctx.dma("pool", wd2b(par), wd2_hbm[f2], deps=prev_wd[par])
            prev_wd[par] = []
            bank = ("w0", "w1")[par]
            ps = pp.get(bank)
            mmd = None
            for c in range(NCI):
                mmd = ctx.emit("pe", lambda t, o=ps[:], l=wd2b(par)[:, c * 128:(c + 1) * 128],
                               r=za_c(c),
                               st=(c == 0), sp=(c == NCI - 1):
                               t.matmul(o, l, r, start=st, stop=sp),
                               deps=[ldw, lza] + (pp.deps(bank) if c == 0 else []),
                               token=(c == NCI - 1))
            prev_wd[par].append(mmd)
            prev_za.append(mmd)
            tev = ctx.emit("dve", lambda v, o=x_c(f2)[:, sl], s=sdp[:, f2:f2 + 1], p=ps[:]:
                           v.scalar_tensor_tensor(o, p, s, o, ALU.mult, ALU.add),
                           deps=[mmd, tv4], token=True)
            pp.consumed(bank, tev)

    ctx.barrier(["pe", "dve", "act"])
    emit_ln(ctx, consts, x_c, x_c)


def np_h128():
    h = np.array([[1.0]], dtype=np.float32)
    while h.shape[0] < 128:
        h = np.block([[h, h], [h, -h]])
    return h


def run_l0_ffn(xin_list, inputs, n_cores=8):
    nc = build_l0_ffn()
    ident, _ = np_consts()
    h128 = np_h128()
    in_maps = []
    for b in range(n_cores):
        in_maps.append({
            "xin": xin_list[b],
            "W_up": inputs["L0_ffn_W_up"], "W_gate": inputs["L0_ffn_W_gate"],
            "W_down": inputs["L0_ffn_W_down"],
            "SU_ug": inputs["L0_ffn_SU_ug"], "SU_dp": inputs["L0_ffn_SU_dp"],
            "Wscale_ug": inputs["L0_ffn_Wscale_ug"], "Wscale_dp": inputs["L0_ffn_Wscale_dp"],
            "c_ident": ident, "c_h128": h128,
        })
    res = run_bass_kernel_spmd(nc, in_maps, core_ids=list(range(n_cores)))
    return [r["xout"] for r in res.results]



H, HD, M, CHUNK = 16, 128, 4, 128
NCH = S // CHUNK
SCALE_L1 = HD ** (-0.5)
EPS_L1 = 1e-6


def build_l1_attn():
    nc = bass.Bass("TRN2", target_bir_lowering=False, debug=False)
    aps = {}
    aps["memories"] = nc.dram_tensor("memories", [M, H, HD, HD], F32, kind="ExternalInput").ap()
    aps["memory_norms"] = nc.dram_tensor("memory_norms", [M, H, HD], F32, kind="ExternalInput").ap()
    aps["w_q"] = nc.dram_tensor("w_q", [D, D], F32, kind="ExternalInput").ap()
    aps["w_k"] = nc.dram_tensor("w_k", [D, D], F32, kind="ExternalInput").ap()
    aps["w_v"] = nc.dram_tensor("w_v", [D, D], F32, kind="ExternalInput").ap()
    aps["w_o"] = nc.dram_tensor("w_o", [D, D], F32, kind="ExternalInput").ap()
    aps["gate"] = nc.dram_tensor("gate", [H], F32, kind="ExternalInput").ap()
    aps["c_ident"] = nc.dram_tensor("c_ident", [128, 128], F32, kind="ExternalInput").ap()
    aps["c_maskT"] = nc.dram_tensor("c_maskT", [128, 128], F32, kind="ExternalInput").ap()
    xin = nc.dram_tensor("xin", [NC, 128, S], BF16, kind="ExternalInput").ap()
    xout = nc.dram_tensor("xout", [NC, 128, S], BF16, kind="ExternalOutput").ap()
    dbg = None
    if os.environ.get("BK_DBG", "0") == "1":
        dbg = {"ao": nc.dram_tensor("d_ao", [NC, 128, S], BF16, kind="ExternalOutput").ap(),
               "sg": nc.dram_tensor("d_sg", [128, 20480], BF16, kind="ExternalOutput").ap()}
    es = ExitStack()
    with es:
        ctx = make_ctx(nc, es)
        consts = Consts(ctx, aps)
        x = ctx.sbuf("big_a", [128, NC * S], BF16)
        x_c = lambda c: x[:, c * S:(c + 1) * S]
        ctx.dma_group("pool", [(x_c(c), xin[c]) for c in range(NC)])
        ctx.barrier()
        emit_l1_attn(ctx, consts, aps, dbg)
        ctx.barrier(["pool"])
        ctx.dma_group("pool", [(xout[c], x_c(c)) for c in range(NC)])
        ctx.barrier(["sp"])
        ctx.build()
    return nc


def emit_l1_attn(ctx, consts, aps, dbg=None):
    nc = ctx.nc
    pp = ctx.psum_pool
    x = ctx.sbuf("big_a", [128, NC * S], BF16)
    x_c = lambda c: x[:, c * S:(c + 1) * S]
    ao = ctx.sbuf("big_b", [128, NC * S], BF16)
    ao_c = lambda c: ao[:, c * S:(c + 1) * S]
    bc = ctx.sbuf("big_c", [128, 20480], BF16)
    wT = lambda par, w: bc[:, w * 2048:(w + 1) * 2048]
    sgq_p = lambda p_: bc[:, 6144 + p_ * 6144: 6144 + p_ * 6144 + 2048]
    sgk_p = lambda p_: bc[:, 8192 + p_ * 6144: 8192 + p_ * 6144 + 2048]
    v_tm_p = lambda p_, c: bc[:, 10240 + p_ * 6144 + c * 128: 10240 + p_ * 6144 + (c + 1) * 128]
    wnat = bc[:, 18432:20480]
    maskT = ctx.sbuf("c_maskT_bf", [128, 128], BF16)
    tmk = ctx.dma("pool", maskT[:], aps["c_maskT"])

    mn_nat = bc[0:M * H, 18432:18944].bitcast(F32)[:, 0:HD]
    mnT = ctx.sbuf("mnT", [128, M * H], BF16)
    t0 = ctx.dma("pool", mn_nat, aps["memory_norms"].rearrange("m h d -> (m h) d"))
    ps1 = pp.get("s1")
    tp0 = ctx.emit("pe", lambda t: t.transpose(ps1[0:128, 0:M * H], mn_nat,
                                               consts.ident_f[0:M * H, 0:M * H]),
                   deps=[t0] + consts.ready + pp.deps("s1"), token=True)
    tmn = ctx.emit("dve", lambda v: v.tensor_copy(mnT[:], ps1[0:128, 0:M * H]), deps=[tp0], token=True)
    pp.consumed("s1", tmn)
    grow = ctx.sbuf("grow", [1, 4 * H], F32)
    g_sig = grow[:, 0:H]
    g1m = grow[:, H:2 * H]
    w5 = grow[:, 2 * H:2 * H + 8]
    graw = grow[:, 3 * H:4 * H]
    tg0 = ctx.dma("pool", graw, aps["gate"].rearrange("(a h) -> a h", a=1))
    tg1 = ctx.emit("act", lambda a: a.activation(g_sig, graw, AF.Sigmoid), deps=[tg0], token=True)
    tg2 = ctx.emit("dve", lambda v: v.tensor_scalar(g1m, g_sig, -1.0, 1.0, ALU.mult, ALU.add),
                   deps=[tg1], token=True)

    mem4 = ctx.scr_b(1)
    qsum = ctx.scr_f(4)
    qsum_bf = ctx.scr_b(6)
    S_f = ctx.scr_f(5)
    S_bf = ctx.scr_b(7)
    elu_t = ctx.scr_f(0)
    elu_e = ctx.scr_f(1)
    elu_r = ctx.scr_f(2)
    wbc5 = ctx.scr_f(3)
    scT = ctx.scr_b(0)
    kp_t = ctx.scr_b(2)
    out_b = ctx.scr_b(3)
    mem_acc = qsum[:, 128:256]
    rn4 = qsum[:, 256:260]
    rd1 = qsum[:, 260:261]

    prev_par = {0: [], 1: []}
    prev_nat = [tmn, tp0]
    ao_done = []
    prev_sg = {0: [], 1: []}
    prev_mem4 = []
    prev_row = []
    prev_state = []
    prev_chunk_dve = []
    prev_outb = []
    for h in range(H):
        par = h % 2
        sgq = sgq_p(par)
        sgk = sgk_p(par)
        v_tm = lambda c, p_=par: v_tm_p(p_, c)
        trt = []
        for w, wap in enumerate((aps["w_q"], aps["w_k"], aps["w_v"])):
            ldn = ctx.dma("pool", wnat, wap[h * 128:(h + 1) * 128, :],
                          deps=prev_nat + prev_par[0])
            tr = ctx.dma_group(("sp", "act", "sp")[w],
                               [(wT(0, w)[:, d_ * 128:(d_ + 1) * 128],
                                 wnat[:, d_ * 128:(d_ + 1) * 128]) for d_ in range(16)],
                               deps=[ldn], transpose=True)
            trt.append(tr)
            prev_nat = [tr]
        prev_par[0] = []
        ldm = ctx.dma("pool", mem4[:].rearrange("p (m e) -> p m e", e=128),
                      aps["memories"].rearrange("m h d e -> h d m e")[h], deps=prev_mem4)

        tqs0 = ctx.emit("dve", lambda v: v.memset(qsum[:, 0:1], 0.0), deps=prev_row, token=True)
        proj_done = []
        elu_prev = []
        for which, (wi, dst) in enumerate(((0, sgq), (1, sgk))):
            for ss in range(NTT):
                sl = slice(ss * TT, (ss + 1) * TT)
                bank = ("w0", "w1")[ss % 2]
                psq = pp.get(bank)
                mmq = None
                for d_ in range(16):
                    mmq = ctx.emit("pe", lambda t, o=psq[:], l=wT(par, wi)[:, d_ * 128:(d_ + 1) * 128],
                                   r=x_c(d_)[:, sl], st=(d_ == 0), sp=(d_ == 15):
                                   t.matmul(o, l, r, start=st, stop=sp),
                                   deps=[trt[wi]] + (pp.deps(bank) if d_ == 0 else []) + prev_sg[par],
                                   token=(d_ == 15))
                if which == 0:
                    tq1 = ctx.emit("dve", lambda v, p=psq[:]: v.reduce_sum(elu_r[:, 0:1], p, axis=AX),
                                   deps=[mmq, tqs0] + elu_prev, token=True)
                    tq2 = ctx.emit("dve", lambda v: v.tensor_add(qsum[:, 0:1], qsum[:, 0:1], elu_r[:, 0:1]),
                                   deps=[tq1], token=True)
                    proj_done.append(tq2)
                e1 = ctx.emit("dve", lambda v, p=psq[:]: v.tensor_scalar(elu_t[:], p, 0.0, None, ALU.min),
                              deps=[mmq] + elu_prev, token=True)
                e2 = ctx.emit("act", lambda a: a.activation(elu_e[:], elu_t[:], AF.Exp),
                              deps=[e1], token=True)
                e3 = ctx.emit("dve", lambda v, p=psq[:]: v.tensor_scalar(elu_r[:], p, 0.0, None, ALU.max),
                              deps=[e1] + ([tq2] if which == 0 else []) + elu_prev, token=True)
                e4 = ctx.emit("dve", lambda v, o=dst[:, sl]: v.tensor_add(o, elu_r[:], elu_e[:]),
                              deps=[e2, e3], token=True)
                pp.consumed(bank, e4)
                proj_done.append(e4)
                elu_prev = [e4]
        for c in range(NCH):
            bank = ("w0", "w1")[c % 2]
            psv = pp.get(bank)
            mmv = None
            for d_ in range(16):
                mmv = ctx.emit("pe", lambda t, o=psv[:, 0:128], l=x_c(d_)[:, c * 128:(c + 1) * 128],
                               r=wT(par, 2)[:, d_ * 128:(d_ + 1) * 128], st=(d_ == 0), sp=(d_ == 15):
                               t.matmul(o, l, r, start=st, stop=sp),
                               deps=[trt[2]] + (pp.deps(bank) if d_ == 0 else []) + prev_sg[par],
                               token=(d_ == 15))
            tve = ctx.emit("dve", lambda v, o=v_tm(c), p=psv[:, 0:128]: v.tensor_copy(o, p),
                           deps=[mmv], token=True)
            pp.consumed(bank, tve)
            proj_done.append(tve)
        tqb = ctx.emit("dve", lambda v: v.tensor_copy(qsum_bf[:, 0:1], qsum[:, 0:1]),
                       deps=proj_done[-1:] + [tq2], token=True)
        psr = pp.get("s2")
        mmr = ctx.emit("pe", lambda t, r=mnT[:, h::H]: t.matmul(psr[0:1, 0:M], qsum_bf[:, 0:1], r,
                                                start=True, stop=True),
                       deps=[tqb, tmn] + pp.deps("s2"), token=True)
        r0 = wbc5[0:1, 8:12]
        r1 = wbc5[0:1, 12:13]
        tr0 = ctx.emit("dve", lambda v: v.tensor_scalar(r0, psr[0:1, 0:M], SCALE_L1 / S, None, ALU.mult),
                       deps=[mmr] + prev_row, token=True)
        pp.consumed("s2", tr0)
        tr1 = ctx.emit("dve", lambda v: v.reduce_max(r1, r0, axis=AX), deps=[tr0], token=True)
        tr2 = ctx.emit("dve", lambda v: v.tensor_scalar(r0, r0, r1, None, ALU.subtract), deps=[tr1], token=True)
        tr3 = ctx.emit("act", lambda a: a.activation(r0, r0, AF.Exp), deps=[tr2], token=True)
        tr4 = ctx.emit("dve", lambda v: v.reduce_sum(r1, r0, axis=AX), deps=[tr3], token=True)
        tr5 = ctx.emit("dve", lambda v: v.reciprocal(r1, r1), deps=[tr4], token=True)
        tr6 = ctx.emit("dve", lambda v, g=g_sig[0:1, h:h + 1]: v.tensor_scalar(w5[:, 0:4], r0, r1, g,
                                                        ALU.mult, ALU.mult), deps=[tr5, tg2], token=True)
        tr7 = ctx.emit("dve", lambda v, g=g1m[0:1, h:h + 1]: v.tensor_copy(w5[:, 4:5], g), deps=[tr6], token=True)
        psb = pp.get("s2")
        mmb = ctx.emit("pe", lambda t: t.matmul(psb[0:128, 0:5], consts.ones_row_f, w5[:, 0:5],
                                                start=True, stop=True),
                       deps=[tr7] + pp.deps("s2"), token=True)
        tbe = ctx.emit("dve", lambda v: v.tensor_copy(wbc5[:, 0:5], psb[0:128, 0:5]),
                       deps=[mmb] + prev_row, token=True)
        pp.consumed("s2", tbe)
        prev_row = [tbe]

        ts0 = ctx.emit("dve", lambda v: v.memset(S_f[:, 0:129], 0.0), deps=prev_state, token=True)
        ts1 = ctx.emit("dve", lambda v: v.memset(S_bf[:, 0:129], 0.0), deps=prev_state, token=True)

        for c in range(NCH):
            csl = slice(c * 128, (c + 1) * 128)
            psm = pp.get("b0")
            mm_m = ctx.emit("pe", lambda t, l=sgq[:, csl]: t.matmul(psm[:], l, mem4[:], start=True, stop=True),
                            deps=[ldm, proj_done[-1]] + pp.deps("b0"), token=True)
            psn = pp.get("s1")
            mm_n = ctx.emit("pe", lambda t, l=sgq[:, csl], r=mnT[:, h::H]: t.matmul(psn[0:128, 0:M], l, r,
                                                                    start=True, stop=True),
                            deps=[tmn] + pp.deps("s1"), token=True)
            tn1 = ctx.emit("dve", lambda v: v.tensor_scalar(rn4[:], psn[0:128, 0:M], EPS_L1, None, ALU.max),
                           deps=[mm_n] + prev_chunk_dve, token=True)
            pp.consumed("s1", tn1)
            tn2 = ctx.emit("dve", lambda v: v.reciprocal(rn4[:], rn4[:]), deps=[tn1], token=True)
            tn3 = ctx.emit("dve", lambda v: v.tensor_mul(rn4[:], rn4[:], wbc5[:, 0:4]),
                           deps=[tn2, tbe], token=True)
            tm0 = ctx.emit("dve", lambda v: v.tensor_scalar(mem_acc[:], psm[:, 0:128], rn4[:, 0:1], None, ALU.mult),
                           deps=[mm_m, tn3], token=True)
            tmx = tm0
            for m in range(1, M):
                tmx = ctx.emit("dve", lambda v, mm=m: v.scalar_tensor_tensor(
                    mem_acc[:], psm[:, mm * 128:(mm + 1) * 128], rn4[:, mm:mm + 1], mem_acc[:],
                    ALU.mult, ALU.add), deps=[tmx], token=True)
            pp.consumed("b0", tmx)
            pss = pp.get("w2")
            mm_s = ctx.emit("pe", lambda t, l=sgk[:, csl], r=sgq[:, csl]:
                            t.matmul(pss[:, 0:128], l, r, start=True, stop=True),
                            deps=pp.deps("w2") + [proj_done[-1]], token=True)
            tms = ctx.emit("dve", lambda v: v.tensor_mul(scT[:, 0:128], pss[:, 0:128], maskT[:]),
                           deps=[mm_s, tmk] + prev_chunk_dve, token=True)
            pp.consumed("w2", tms)
            pst = pp.get("tp")
            mm_t = ctx.emit("pe", lambda t, i=sgk[:, csl]: t.transpose(pst[:, 0:128], i, consts.ident_bf[:]),
                            deps=pp.deps("tp") + [proj_done[-1]], token=True)
            tkt = ctx.emit("act", lambda a: a.copy(kp_t[:, 0:128], pst[:, 0:128]),
                           deps=[mm_t] + prev_outb, token=True)
            pp.consumed("tp", tkt)
            psu = pp.get("w3")
            mm_i = None
            if c > 0:
                mm_i = ctx.emit("pe", lambda t, l=sgq[:, csl]: t.matmul(psu[:, 0:128], l, S_bf[:, 0:128],
                                                                        start=True, stop=False),
                                deps=pp.deps("w3") + prev_state, token=True)
            mm_a = ctx.emit("pe", lambda t, r=v_tm(c), st=(c == 0): t.matmul(psu[:, 0:128], scT[:, 0:128], r,
                                                     start=st, stop=True),
                            deps=[tms] + ([mm_i] if mm_i else pp.deps("w3")), token=True)
            psd = pp.get("s1")
            mm_d1 = None
            if c > 0:
                mm_d1 = ctx.emit("pe", lambda t, l=sgq[:, csl]: t.matmul(psd[:, 4:5], l, S_bf[:, 128:129],
                                                                         start=True, stop=False),
                                 deps=pp.deps("s1") + prev_state, token=True)
            mm_d2 = ctx.emit("pe", lambda t, st=(c == 0): t.matmul(psd[:, 4:5], scT[:, 0:128], consts.ones_col_bf[:],
                                                      start=st, stop=True),
                             deps=[tms] + ([mm_d1] if mm_d1 else pp.deps("s1")) + consts.ready, token=True)
            td1 = ctx.emit("dve", lambda v: v.tensor_scalar(rd1[:], psd[:, 4:5], EPS_L1, None, ALU.max),
                           deps=[mm_d2] + prev_chunk_dve, token=True)
            pp.consumed("s1", td1)
            td2 = ctx.emit("dve", lambda v: v.reciprocal(rd1[:], rd1[:]), deps=[td1], token=True)
            td3 = ctx.emit("dve", lambda v: v.tensor_mul(rd1[:], rd1[:], wbc5[:, 4:5]),
                           deps=[td2, tbe], token=True)
            tob = ctx.emit("dve", lambda v: v.scalar_tensor_tensor(out_b[:, 0:128], psu[:, 0:128],
                                                                   rd1[:, 0:1], mem_acc[:],
                                                                   ALU.mult, ALU.add),
                           deps=[mm_a, td3, tmx], token=True)
            pp.consumed("w3", tob)
            mm_o = ctx.emit("pe", lambda t: t.transpose(pst[:, 128:256], out_b[:, 0:128], consts.ident_bf[:]),
                            deps=[tob] + pp.deps("tp"), token=True)
            toe = ctx.emit("act", lambda a, o=ao_c(h)[:, csl]: a.copy(o, pst[:, 128:256]),
                           deps=[mm_o], token=True)
            pp.consumed("tp", toe)
            ao_done.append(toe)
            prev_outb = [toe]
            psk = pp.get("s2")
            mm_k = ctx.emit("pe", lambda t, r=v_tm(c): t.matmul(psk[:, 0:128], kp_t[:, 0:128], r,
                                                     start=True, stop=True),
                            deps=[tkt] + pp.deps("s2"), token=True)
            tz1 = ctx.emit("dve", lambda v, i=sgk[:, csl]: v.reduce_sum(elu_r[:, 0:1], i, axis=AX),
                           deps=[proj_done[-1]] + prev_state + prev_chunk_dve, token=True)
            ts_a = ctx.emit("dve", lambda v: v.tensor_add(S_f[:, 0:128], S_f[:, 0:128], psk[:, 0:128]),
                            deps=[mm_k] + ([mm_i] if mm_i else []) + ([mm_d1] if mm_d1 else []), token=True)
            pp.consumed("s2", ts_a)
            ts_z = ctx.emit("dve", lambda v: v.tensor_add(S_f[:, 128:129], S_f[:, 128:129], elu_r[:, 0:1]),
                            deps=[tz1], token=True)
            ts_c = ctx.emit("dve", lambda v: v.tensor_copy(S_bf[:, 0:129], S_f[:, 0:129]),
                            deps=[ts_a, ts_z], token=True)
            prev_state = [ts_c]
            prev_chunk_dve = [ts_c]
        prev_sg[par] = [ts_c, mm_k]
        prev_par[0] = [mmv]
        prev_mem4 = [tmx]

    if dbg is not None:
        ctx.barrier(["pool"])
        ao3 = ao.rearrange("p (c s) -> p c s", c=NC)
        for c in range(NC):
            ctx.dma("pool", dbg["ao"][c], ao3[:, c])
        ctx.dma("pool", dbg["sg"], bc[:])
    ctx.barrier(["pe", "dve", "act", "pool", "sp"])
    prev_nat2 = []
    prev_wo = {0: [], 1: []}
    for fo in range(NC):
        par = fo % 2
        ldn = ctx.dma("pool", wnat, aps["w_o"][fo * 128:(fo + 1) * 128, :], deps=prev_nat2)
        wot = (bc[:, 0:2048], bc[:, 2048:4096])[par]
        tr = ctx.dma_group("sp", [(wot[:, d_ * 128:(d_ + 1) * 128],
                                   wnat[:, d_ * 128:(d_ + 1) * 128]) for d_ in range(16)],
                           deps=[ldn] + prev_wo[par], transpose=True)
        prev_nat2 = [tr]
        prev_wo[par] = []
        for tt in range(NTT):
            sl = slice(tt * TT, (tt + 1) * TT)
            bank = ("w0", "w1")[tt % 2]
            pso = pp.get(bank)
            mmo = None
            for d_ in range(16):
                mmo = ctx.emit("pe", lambda t, o=pso[:], l=wot[:, d_ * 128:(d_ + 1) * 128],
                               r=ao_c(d_)[:, sl], st=(d_ == 0), sp=(d_ == 15):
                               t.matmul(o, l, r, start=st, stop=sp),
                               deps=[tr] + (pp.deps(bank) if d_ == 0 else []), token=(d_ == 15))
            prev_wo[par].append(mmo)
            tev = ctx.emit("dve", lambda v, o=x_c(fo)[:, sl], p=pso[:]: v.tensor_add(o, o, p),
                           deps=[mmo], token=True)
            pp.consumed(bank, tev)

    ctx.barrier(["pe", "dve", "act"])
    emit_ln(ctx, consts, x_c, x_c)


def np_maskT():
    j = np.arange(128)[:, None]
    i = np.arange(128)[None, :]
    return (i >= j).astype(np.float32)


def run_l1_attn(xin_list, inputs, n_cores=8):
    nc = build_l1_attn()
    ident, _ = np_consts()
    in_maps = []
    for b in range(n_cores):
        in_maps.append({
            "xin": xin_list[b],
            "memories": inputs["L1_attn_memories"], "memory_norms": inputs["L1_attn_memory_norms"],
            "w_q": inputs["L1_attn_w_q"], "w_k": inputs["L1_attn_w_k"],
            "w_v": inputs["L1_attn_w_v"], "w_o": inputs["L1_attn_w_o"],
            "gate": inputs["L1_attn_gate"],
            "c_ident": ident, "c_maskT": np_maskT(),
        })
    res = run_bass_kernel_spmd(nc, in_maps, core_ids=list(range(n_cores)))
    return [r["xout"] for r in res.results]



FFN = 1408
NEXP = 8
NCF = 11


def build_l1_moe():
    nc = bass.Bass("TRN2", target_bir_lowering=False, debug=False)
    aps = {}
    aps["wg"] = nc.dram_tensor("wg", [NEXP, D], F32, kind="ExternalInput").ap()
    aps["w1"] = nc.dram_tensor("w1", [NEXP, D, 2 * FFN], F32, kind="ExternalInput").ap()
    aps["w2"] = nc.dram_tensor("w2", [NEXP, FFN, D], F32, kind="ExternalInput").ap()
    aps["c_ident"] = nc.dram_tensor("c_ident", [128, 128], F32, kind="ExternalInput").ap()
    xin = nc.dram_tensor("xin", [NC, 128, S], BF16, kind="ExternalInput").ap()
    out = nc.dram_tensor("out", [S, D], F32, kind="ExternalOutput").ap()
    acc_hbm = nc.dram_tensor("acc_hbm", [NC, 128, S], F32, kind="Internal").ap()
    crow_hbm = nc.dram_tensor("crow_hbm", [NEXP, S], F32, kind="Internal").ap()
    es = ExitStack()
    with es:
        ctx = make_ctx(nc, es)
        consts = Consts(ctx, aps)
        x = ctx.sbuf("big_a", [128, NC * S], BF16)
        x_c = lambda c: x[:, c * S:(c + 1) * S]
        ctx.dma_group("pool", [(x_c(c), xin[c]) for c in range(NC)])
        ctx.barrier()
        emit_l1_moe(ctx, consts, aps, acc_hbm, crow_hbm)
        emit_final_out(ctx, consts, out, acc_hbm)
        ctx.barrier(["sp"])
        ctx.build()
    return nc


def emit_l1_moe(ctx, consts, aps, acc_hbm, crow_hbm):
    nc = ctx.nc
    pp = ctx.psum_pool
    x = ctx.sbuf("big_a", [128, NC * S], BF16)
    x_c = lambda c: x[:, c * S:(c + 1) * S]
    y = ctx.sbuf("big_b", [128, NC * S], BF16)
    y_c = lambda j, tt: y[:, j * S + tt * TT: j * S + (tt + 1) * TT]
    bc = ctx.sbuf("big_c", [128, 20480], BF16)
    wgT = bc[:, 8704:8832]
    ce_all = bc[:, 16384:20480].bitcast(F32)
    rows = ctx.sbuf("scr_row", [1, 3 * TT], F32)
    ce_row = rows[0:1, 0:TT]

    wg_nat = bc[0:8, 0:2 * D].bitcast(F32)
    t0 = ctx.dma("pool", wg_nat, aps["wg"])
    ps1 = pp.get("s1")
    wgt_done = []
    for c in range(NC):
        tp0 = ctx.emit("pe", lambda t, i=wg_nat[:, c * 128:(c + 1) * 128]:
                       t.transpose(ps1[0:128, 0:8], i, consts.ident_f[0:8, 0:8]),
                       deps=[t0] + consts.ready + pp.deps("s1"), token=True)
        te0 = ctx.emit("dve", lambda v, o=wgT[:, c * 8:(c + 1) * 8]:
                       v.tensor_copy(o, ps1[0:128, 0:8]), deps=[tp0], token=True)
        pp.consumed("s1", te0)
        wgt_done.append(te0)
    sf = ctx.scr_f(0)
    scores = sf[:, 0:8]
    mcol = sf[:, 8:9]
    m1 = sf[:, 9:10]
    m2 = sf[:, 10:11]
    dn = sf[:, 11:12]
    is1 = sf[:, 16:24]
    msk = sf[:, 24:32]
    Cc = sf[:, 32:40]
    stage8 = bc[0:8, 2 * D * 2:2 * D * 2 + 512].bitcast(F32)[:, 0:128]
    prev_g = []
    crow_toks = []
    for tc in range(NC):
        psl = pp.get("s2")
        mml = None
        for d_ in range(NC):
            mml = ctx.emit("pe", lambda t, o=psl[0:128, 0:8], l=x_c(d_)[:, tc * 128:(tc + 1) * 128],
                           r=wgT[:, d_ * 8:(d_ + 1) * 8], st=(d_ == 0), sp=(d_ == NC - 1):
                           t.matmul(o, l, r, start=st, stop=sp),
                           deps=[wgt_done[d_]] + (pp.deps("s2") if d_ == 0 else []),
                           token=(d_ == NC - 1))
        g1 = ctx.emit("dve", lambda v: v.tensor_copy(scores, psl[0:128, 0:8]), deps=[mml] + prev_g, token=True)
        pp.consumed("s2", g1)
        g2 = ctx.emit("dve", lambda v: v.reduce_max(mcol, scores, axis=AX), deps=[g1], token=True)
        g3 = ctx.emit("dve", lambda v: v.tensor_scalar(scores, scores, mcol, None, ALU.subtract),
                      deps=[g2], token=True)
        g4 = ctx.emit("act", lambda a: a.activation(scores, scores, AF.Exp), deps=[g3], token=True)
        g5 = ctx.emit("dve", lambda v: v.reduce_sum(mcol, scores, axis=AX), deps=[g4], token=True)
        g6 = ctx.emit("dve", lambda v: v.reciprocal(mcol, mcol), deps=[g5], token=True)
        g7 = ctx.emit("dve", lambda v: v.tensor_scalar(scores, scores, mcol, None, ALU.mult),
                      deps=[g6], token=True)
        g8 = ctx.emit("dve", lambda v: v.reduce_max(m1, scores, axis=AX), deps=[g7], token=True)
        g9 = ctx.emit("dve", lambda v: v.tensor_scalar(is1, scores, m1, None, ALU.is_ge), deps=[g8], token=True)
        g10 = ctx.emit("dve", lambda v: v.scalar_tensor_tensor(msk, is1, -1e30, scores, ALU.mult, ALU.add),
                       deps=[g9], token=True)
        g11 = ctx.emit("dve", lambda v: v.reduce_max(m2, msk, axis=AX), deps=[g10], token=True)
        g12 = ctx.emit("dve", lambda v: v.tensor_scalar(is1, scores, m2, None, ALU.is_ge), deps=[g11, g10], token=True)
        g13 = ctx.emit("dve", lambda v: v.tensor_add(dn, m1, m2), deps=[g11], token=True)
        g14 = ctx.emit("dve", lambda v: v.tensor_scalar(dn, dn, 1e-20, None, ALU.add), deps=[g13], token=True)
        g15 = ctx.emit("dve", lambda v: v.reciprocal(dn, dn), deps=[g14], token=True)
        g16 = ctx.emit("dve", lambda v: v.tensor_mul(Cc, scores, is1), deps=[g12, g15], token=True)
        g17 = ctx.emit("dve", lambda v: v.tensor_scalar(Cc, Cc, dn, None, ALU.mult), deps=[g16], token=True)
        pst = pp.get("s1")
        g19 = ctx.emit("pe", lambda t: t.transpose(pst[0:8, 0:128], Cc, consts.ident_f[:]),
                       deps=[g17] + pp.deps("s1"), token=True)
        g20 = ctx.emit("dve", lambda v: v.tensor_copy(stage8, pst[0:8, 0:128]), deps=[g19] + prev_g, token=True)
        pp.consumed("s1", g20)
        tdc = ctx.dma("pool", crow_hbm[:, tc * 128:(tc + 1) * 128], stage8, deps=[g20])
        prev_g = [tdc]
        crow_toks.append(tdc)
    prev_ce = list(crow_toks)

    blkA = lambda d_: bc[:, d_ * 512:(d_ + 1) * 512]
    blkB = lambda d_: bc[:, 8192 + d_ * 512:8192 + (d_ + 1) * 512]
    w2blk = lambda c: bc[:, c * 512:(c + 1) * 512]
    sa_f0, sa_f1 = ctx.scr_f(1), ctx.scr_f(3)
    ty_f0, ty_f1 = ctx.scr_f(2), ctx.scr_f(4)
    ev0 = ctx.scr_f(0)
    ev1 = ctx.scr_f(5)
    prev_blk = [mml]
    prev_w2 = []
    prev_y1 = {0: [], 1: []}
    prev_y2 = {0: [], 1: []}
    prev_acc = {0: [], 1: []}
    acc_order = []
    for e in range(NEXP):
        ce_done = []
        for tt in range(NTT):
            tcr = ctx.dma("pool", ce_row, crow_hbm[e, tt * TT:(tt + 1) * TT].rearrange("(a t) -> a t", a=1),
                          deps=prev_ce)
            psb = pp.get("b0")
            mcb = ctx.emit("pe", lambda t: t.matmul(psb[:], consts.ones_row_f, ce_row,
                                                    start=True, stop=True),
                           deps=[tcr] + pp.deps("b0"), token=True)
            tce = ctx.emit("dve", lambda v, o=ce_all[:, tt * TT:(tt + 1) * TT]:
                           v.tensor_copy(o, psb[:]), deps=[mcb] + prev_ce, token=True)
            pp.consumed("b0", tce)
            ce_done.append(tce)
            prev_ce = [tce]
        up_done = []
        for wp in range(3):
            wa = 512 if wp < 2 else 384
            nf = wa // 128
            lda = ctx.dma_group("pool", [
                (blkA(d_)[:, 0:wa], aps["w1"][e, d_ * 128:(d_ + 1) * 128, wp * 512:wp * 512 + wa])
                for d_ in range(NC)], deps=prev_blk)
            ldb = ctx.dma_group("pool", [
                (blkB(d_)[:, 0:wa], aps["w1"][e, d_ * 128:(d_ + 1) * 128, FFN + wp * 512:FFN + wp * 512 + wa])
                for d_ in range(NC)])
            prev_blk = []
            for f1 in range(nf):
                j = wp * 4 + f1
                for tt in range(NTT):
                    sl = slice(tt * TT, (tt + 1) * TT)
                    psA = pp.get("w0")
                    psB = pp.get("w1")
                    mma = mmb = None
                    for d_ in range(NC):
                        mma = ctx.emit("pe", lambda t, o=psA[:], l=blkA(d_)[:, f1 * 128:(f1 + 1) * 128],
                                       r=x_c(d_)[:, sl], st=(d_ == 0), sp=(d_ == NC - 1):
                                       t.matmul(o, l, r, start=st, stop=sp),
                                       deps=[lda] + (pp.deps("w0") if d_ == 0 else []),
                                       token=(d_ == NC - 1))
                    for d_ in range(NC):
                        mmb = ctx.emit("pe", lambda t, o=psB[:], l=blkB(d_)[:, f1 * 128:(f1 + 1) * 128],
                                       r=x_c(d_)[:, sl], st=(d_ == 0), sp=(d_ == NC - 1):
                                       t.matmul(o, l, r, start=st, stop=sp),
                                       deps=[ldb] + (pp.deps("w1") if d_ == 0 else []),
                                       token=(d_ == NC - 1))
                    prev_blk.extend([mma, mmb])
                    q = (j * NTT + tt) % 2
                    sa_f = (sa_f0, sa_f1)[q]
                    ty_f = (ty_f0, ty_f1)[q]
                    tsa = ctx.emit("act", lambda a, p=psA[:], o=sa_f[:]: a.activation(o, p, AF.Silu),
                                   deps=[mma] + prev_y1[q], token=True)
                    pp.consumed("w0", tsa)
                    tty = ctx.emit("dve", lambda v, p=psB[:], o=ty_f[:], i=sa_f[:]: v.tensor_mul(o, i, p),
                                   deps=[tsa, mmb] + prev_y2[q], token=True)
                    pp.consumed("w1", tty)
                    tyb = ctx.emit("dve", lambda v, o=y_c(j, tt), cb=ce_all[:, sl], i=ty_f[:]:
                                   v.tensor_mul(o, i, cb),
                                   deps=[tty, ce_done[tt]], token=True)
                    prev_y1[q] = [tty]
                    prev_y2[q] = [tyb]
                    up_done.append(tyb)
        for fw in range(4):
            ldw = ctx.dma_group("pool", [
                (w2blk(c), aps["w2"][e, c * 128:(c + 1) * 128, fw * 512:(fw + 1) * 512])
                for c in range(NCF)], deps=prev_blk + prev_w2)
            prev_blk = []
            prev_w2 = []
            for f1 in range(4):
                f2 = fw * 4 + f1
                for tt in range(NTT):
                    sl = slice(tt * TT, (tt + 1) * TT)
                    par = (f1 * NTT + tt) % 2
                    bank = ("w0", "w1")[par]
                    psd = pp.get(bank)
                    mmd = None
                    for c in range(NCF):
                        mmd = ctx.emit("pe", lambda t, o=psd[:], l=w2blk(c)[:, f1 * 128:(f1 + 1) * 128],
                                       r=y_c(c, tt), st=(c == 0), sp=(c == NCF - 1):
                                       t.matmul(o, l, r, start=st, stop=sp),
                                       deps=[ldw, up_done[-1]] + (pp.deps(bank) if c == 0 else []),
                                       token=(c == NCF - 1))
                    prev_w2.append(mmd)
                    ev = (ev0, ev1)[par]
                    tev = ctx.emit("dve", lambda v, o=ev[:], p=psd[:]: v.tensor_copy(o, p),
                                   deps=[mmd] + prev_acc[par], token=True)
                    pp.consumed(bank, tev)
                    tacc = ctx.dma("pool", acc_hbm[f2][:, sl], ev[:], deps=[tev], accum=(e > 0))
                    prev_acc[par] = [tacc]
                    acc_order.append(tacc)
    return acc_order


def emit_final_out(ctx, consts, out_ap, acc_hbm):
    pp = ctx.psum_pool
    x = ctx.sbuf("big_a", [128, NC * S], BF16)
    x_c = lambda c: x[:, c * S:(c + 1) * S]
    ctx.barrier(["pe", "dve", "act", "pool", "sp"])
    ev0 = ctx.scr_f(1)
    ev1 = ctx.scr_f(2)
    prev_ld = {0: [], 1: []}
    for f2 in range(NC):
        for tt in range(NTT):
            sl = slice(tt * TT, (tt + 1) * TT)
            par = tt % 2
            ev = (ev0, ev1)[par]
            tld = ctx.dma("pool", ev[:], acc_hbm[f2][:, sl], deps=prev_ld[par])
            tad = ctx.emit("dve", lambda v, o=x_c(f2)[:, sl], e_=ev[:]: v.tensor_add(o, o, e_),
                           deps=[tld], token=True)
            prev_ld[par] = [tad]
    ctx.barrier(["pe", "dve", "act"])
    emit_ln(ctx, consts, x_c, x_c)
    ctx.barrier(["pe", "act"])
    bc = ctx.sbuf("big_c", [128, 20480], BF16)
    stage = bc[:, 0:2 * D].bitcast(F32)
    prev_st = []
    for sc in range(16):
        ev_toks = []
        for half in range(2):
            tpb = pp.get("tp")
            ttp = None
            for k in range(8):
                dc = half * 8 + k
                ttp = ctx.emit("pe", lambda t, i=x_c(dc)[:, sc * 128:(sc + 1) * 128],
                               o=tpb[:, k * 128:(k + 1) * 128]:
                               t.transpose(o, i, consts.ident_bf[:]),
                               deps=consts.ready + (pp.deps("tp") if k == 0 else []),
                               token=(k == 7))
            tev = ctx.emit("act", lambda a,
                           o=stage[:, half * 1024:(half + 1) * 1024].rearrange("p (c f) -> p c f", f=128),
                           i=tpb[:].rearrange("p (c f) -> p c f", f=128):
                           a.copy(o, i),
                           deps=[ttp] + prev_st, token=True)
            pp.consumed("tp", tev)
            ev_toks.append(tev)
        prev_st = [ctx.dma("pool", out_ap[sc * 128:(sc + 1) * 128, :], stage[:], deps=ev_toks)]
    ctx.barrier(["sp"])


def run_l1_moe(xin_list, inputs, n_cores=8):
    nc = build_l1_moe()
    ident, _ = np_consts()
    in_maps = []
    for b in range(n_cores):
        in_maps.append({
            "xin": xin_list[b],
            "wg": inputs["L1_ffn_wg"], "w1": inputs["L1_ffn_w1"], "w2": inputs["L1_ffn_w2"],
            "c_ident": ident,
        })
    res = run_bass_kernel_spmd(nc, in_maps, core_ids=list(range(n_cores)))
    return [r["out"] for r in res.results]
